# version 14
# baseline (speedup 1.0000x reference)
import contextlib
import numpy as np

import concourse.bass as bass
import concourse.mybir as mybir
from concourse.bass_utils import run_bass_kernel_spmd

F32 = mybir.dt.float32
BF16 = mybir.dt.bfloat16
AF = mybir.ActivationFunctionType
ALU = mybir.AluOpType
AX = mybir.AxisListType

N_CORES = 8
D = 1024
T_SEQ = 2048
N_TOK = 4096
GRP = 512
N_GRP = 8
N_EXP = 8
I_FFN = 4096
LAT = 512
HD = 64
N_PAIR = 8
CSEL = 256
HUGE = 16777216.0


class Ch:
    __slots__ = ("name", "n")

    def __init__(self, name):
        self.name = name
        self.n = 0


class Prog:
    def __init__(self):
        self.ops = []

    def op(self, eng, fn, waits=(), incs=()):
        wd = {}
        for w in waits:
            if w is None:
                continue
            ch, v = w
            if v > 0 and (ch not in wd or wd[ch] < v):
                wd[ch] = v
        rec = []
        for ch, k in incs:
            ch.n += k
            rec.append((ch, k))
        self.ops.append((eng, fn, tuple(wd.items()), tuple(rec)))

    def check_deadlock(self):
        by_eng = {e: [] for e in ("tensor", "scalar", "vector", "gpsimd", "sync")}
        for i, (eng, fn, waits, incs) in enumerate(self.ops):
            by_eng[eng].append((i, waits, incs))
        ptr = {e: 0 for e in by_eng}
        val = {}
        total = sum(len(v) for v in by_eng.values())
        done = 0
        progress = True
        while progress:
            progress = False
            for e, lst in by_eng.items():
                while ptr[e] < len(lst):
                    i, waits, incs = lst[ptr[e]]
                    if any(val.get(ch, 0) < v for ch, v in waits):
                        break
                    for ch, k in incs:
                        val[ch] = val.get(ch, 0) + k
                    ptr[e] += 1
                    done += 1
                    progress = True
        if done != total:
            msgs = []
            for e, lst in by_eng.items():
                if ptr[e] < len(lst):
                    i, waits, incs = lst[ptr[e]]
                    missing = [(ch.name, v, val.get(ch, 0)) for ch, v in waits
                               if val.get(ch, 0) < v]
                    msgs.append(f"{e}@op{i} needs {missing}")
            raise RuntimeError("DEADLOCK: " + "; ".join(msgs))


class Builder:
    def __init__(self):
        self.nc = bass.Bass("TRN2", target_bir_lowering=False, debug=False)
        self.P = Prog()
        self.ch = {n: Ch(n) for n in
                   ("pe", "act", "dve", "pool", "mld", "cvl", "cst", "wup",
                    "wdn", "x", "out")}
        self.sb = {}
        self.ps = {}
        self.dram = {}
        self.marks = {}
        self.tring_free = [None, None]
        self.tring_i = 0
        self.bank_free = {k: None for k in ("F0", "F1", "F2", "F3", "F4", "F5")}

    def declare(self):
        nc, dr = self.nc, self.dram
        dr["x"] = nc.dram_tensor("x", [N_TOK, D], F32, kind="ExternalInput")
        dr["wcd"] = nc.dram_tensor("d_wcd", [D, LAT], F32, kind="ExternalInput")
        dr["wq"] = nc.dram_tensor("d_wq", [LAT, D], F32, kind="ExternalInput")
        dr["wkv"] = nc.dram_tensor("d_wkv", [LAT, 2 * HD], F32, kind="ExternalInput")
        dr["wproj"] = nc.dram_tensor("d_wproj", [D, D], F32, kind="ExternalInput")
        dr["gw"] = nc.dram_tensor("gw", [N_EXP, D], F32, kind="ExternalInput")
        dr["eb"] = nc.dram_tensor("eb", [N_EXP], F32, kind="ExternalInput")
        dr["wup"] = nc.dram_tensor("wup", [N_EXP, I_FFN, D], F32, kind="ExternalInput")
        dr["wdn"] = nc.dram_tensor("wdn", [N_EXP, D, I_FFN], F32, kind="ExternalInput")
        dr["eyeb"] = nc.dram_tensor("d_eyeb", [128, 128], BF16, kind="ExternalInput")
        dr["eyef"] = nc.dram_tensor("d_eyef", [128, 128], F32, kind="ExternalInput")
        dr["maskc"] = nc.dram_tensor("d_maskc", [128, 896], BF16, kind="ExternalInput")
        dr["dup64"] = nc.dram_tensor("d_dup64", [64, 128], BF16, kind="ExternalInput")
        dr["jrow"] = nc.dram_tensor("d_jrow", [128, CSEL], F32,
                                    kind="ExternalInput")
        dr["out"] = nc.dram_tensor("out", [N_TOK, D], F32, kind="ExternalOutput")
        dr["up_s"] = nc.dram_tensor("up_s", [N_EXP, 8, 128, 4096], BF16)
        dr["dn_s"] = nc.dram_tensor("dn_s", [N_EXP, 8, 128, 4096], BF16)

    def alloc(self, es):
        nc, S = self.nc, self.sb

        def sbuf(name, shape, dt):
            S[name] = es.enter_context(nc.sbuf_tensor(name, shape, dt))

        sbuf("eyeb", [128, 128], BF16)
        sbuf("eyef", [128, 128], F32)
        sbuf("maskc", [128, 896], BF16)
        sbuf("dup64", [64, 128], BF16)
        sbuf("ones1f", [1, 128], F32)
        sbuf("onesb", [128, 64], BF16)
        sbuf("onescol", [128, 1], BF16)
        sbuf("gate_wT", [128, 64], BF16)
        sbuf("ebrow", [1, N_EXP], F32)
        sbuf("bias_b", [128, N_EXP], F32)
        sbuf("wcd", [128, 4096], BF16)
        sbuf("wq", [128, 4096], BF16)
        sbuf("wkv", [128, 512], BF16)
        sbuf("wproj", [128, 8192], BF16)
        sbuf("k_dup", [128, T_SEQ], BF16)
        sbuf("v_aug", [128, 16 * 65], BF16)
        sbuf("xn", [128, 8 * GRP], BF16)
        sbuf("latent", [128, 4 * GRP], BF16)
        sbuf("routg", [128, 160], F32)
        sbuf("mrow", [8, GRP], F32)
        sbuf("posrow", [8, GRP], F32)
        sbuf("ones8", [8, GRP], F32)
        sbuf("carry", [8, 1], F32)
        sbuf("ptok", [128, 32], F32)
        sbuf("jrow", [128, CSEL], F32)
        sbuf("xn_tok", [128, 4 * D], BF16)
        sbuf("Pg", [128, 4 * CSEL], BF16)
        sbuf("Pw", [128, 4 * CSEL], BF16)
        sbuf("PTw", [128, 2 * GRP], BF16)
        sbuf("gxn", [128, 8 * CSEL], BF16)
        sbuf("ysel", [128, 8 * CSEL], BF16)
        sbuf("yselT", [128, 2 * D], BF16)
        sbuf("arena", [128, 8192], BF16)
        sbuf("out_ffn", [128, 4 * D], F32)
        sbuf("up_buf0", [128, 2048], BF16)
        sbuf("up_buf1", [128, 2048], BF16)
        sbuf("up_buf2", [128, 2048], BF16)
        sbuf("up_buf3", [128, 2048], BF16)
        sbuf("dn_buf0", [128, 2048], BF16)
        sbuf("dn_buf1", [128, 2048], BF16)
        sbuf("dn_buf2", [128, 2048], BF16)
        sbuf("dn_buf3", [128, 2048], BF16)
        sbuf("tmp0", [128, 128], F32)
        sbuf("attn", [128, 8 * GRP], BF16)
        sbuf("y_g", [128, 8 * GRP], BF16)
        sbuf("q0", [128, GRP], BF16)
        sbuf("pT0", [128, 1024], BF16)
        sbuf("pT1", [128, 1024], BF16)
        sbuf("rz", [128, 512], BF16)
        sbuf("zb", [128, GRP], BF16)
        sbuf("kvb", [128, GRP], BF16)
        sbuf("x0", [128, D], F32)
        sbuf("sq0", [128, D], BF16)
        sbuf("lnstat", [128, 16], F32)
        sbuf("epsv", [128, 1], F32)
        sbuf("conv0", [128, 4096], BF16)
        sbuf("conv1", [128, 4096], BF16)
        sbuf("ev", [128, 4096], BF16)

        PS = self.ps
        PS["FU"] = es.enter_context(nc.psum_tensor("FU", [128, 1024], F32))
        PS["FD"] = es.enter_context(nc.psum_tensor("FD", [128, 1024], F32))
        for k in ("F4", "F5"):
            PS[k] = es.enter_context(nc.psum_tensor(k, [128, 512], F32))
        PS["TB"] = es.enter_context(nc.psum_tensor("TB", [128, 2048], BF16))
        self.sems = {n: es.enter_context(nc.semaphore("s_" + n)) for n in self.ch}

    def mm(self, out_ap, lhsT, rhs, start, stop, waits=(), tile_position=None):
        ch = self.ch["pe"]

        def fn(eng, o=out_ap, l=lhsT, r=rhs, a=start, b=stop, tp=tile_position):
            return eng.matmul(o, l, r, start=a, stop=b, tile_position=tp)
        self.P.op("tensor", fn, waits=waits, incs=((ch, 1),))
        return ch.n

    def tr(self, out_ap, in_ap, ident, waits=()):
        ch = self.ch["pe"]

        def fn(eng, o=out_ap, i=in_ap, d=ident):
            return eng.transpose(o, i, d)
        self.P.op("tensor", fn, waits=waits, incs=((ch, 1),))
        return ch.n

    def act(self, out_ap, in_ap, func, waits=(), bias=None, scale=None):
        ch = self.ch["act"]

        def fn(eng, o=out_ap, i=in_ap, f=func, b=bias, s=scale):
            kw = {}
            if b is not None:
                kw["bias"] = b
            if s is not None:
                kw["scale"] = s
            return eng.activation(o, i, f, **kw)
        self.P.op("scalar", fn, waits=waits, incs=((ch, 1),))
        return ch.n

    def acopy(self, out_ap, in_ap, waits=()):
        ch = self.ch["act"]

        def fn(eng, o=out_ap, i=in_ap):
            return eng.copy(o, i)
        self.P.op("scalar", fn, waits=waits, incs=((ch, 1),))
        return ch.n

    def dve(self, kind, waits=(), allow_lp=False, **kw):
        ch = self.ch["dve"]
        nc = self.nc

        def fn(eng, kind=kind, kw=kw, allow_lp=allow_lp):
            if allow_lp:
                with nc.allow_low_precision(reason="bf16 1/Z: feeds bf16 matmul"):
                    return getattr(eng, kind)(**kw)
            return getattr(eng, kind)(**kw)
        self.P.op("vector", fn, waits=waits, incs=((ch, 1),))
        return ch.n

    def pool(self, kind, waits=(), **kw):
        ch = self.ch["pool"]

        def fn(eng, kind=kind, kw=kw):
            return getattr(eng, kind)(**kw)
        self.P.op("gpsimd", fn, waits=waits, incs=((ch, 1),))
        return ch.n

    def dma(self, queue, chname, out_ap, in_ap, waits=()):
        ch = self.ch[chname]

        def fn(eng, o=out_ap, i=in_ap):
            return eng.dma_start(out=o, in_=i)
        self.P.op(queue, fn, waits=waits, incs=((ch, 16),))
        return ch.n

    def barrier(self, eng, waits):
        self.P.op(eng, None, waits=waits)

    def tbank(self):
        slot = self.tring_i % 2
        self.tring_i += 1
        bank = self.ps["TB"][:, slot * 1024:(slot + 1) * 1024]
        w = self.tring_free[slot]
        self.tring_free[slot] = None
        return bank, slot, ([w] if w is not None else [])

    def tbank_release(self, slot, wait):
        self.tring_free[slot] = wait

    def tbank_pair(self):
        if self.tring_i % 2 == 1:
            _, s, w = self.tbank()
            self.tbank_release(s, w[0] if w else None)
        _, s0, w0 = self.tbank()
        _, s1, w1 = self.tbank()
        assert s0 == 0 and s1 == 1
        return w0 + w1

    _FBVIEW = {"F0": ("FU", 0), "F1": ("FU", 1), "F2": ("FD", 0), "F3": ("FD", 1)}

    def fb(self, name):
        w = self.bank_free[name]
        self.bank_free[name] = None
        if name in self._FBVIEW:
            t, half = self._FBVIEW[name]
            return self.ps[t][:, half * 512:(half + 1) * 512], ([w] if w is not None else [])
        return self.ps[name], ([w] if w is not None else [])

    def fb_release(self, name, wait):
        self.bank_free[name] = wait

    def emit_prologue(self):
        S, D_, ch = self.sb, self.dram, self.ch
        loads = [
            (S["eyeb"][:, :], D_["eyeb"].ap()),
            (S["eyef"][:, :], D_["eyef"].ap()),
            (S["maskc"][:, :], D_["maskc"].ap()),
            (S["dup64"][:, :], D_["dup64"].ap()),
            (S["jrow"][:, :], D_["jrow"].ap()),
            (S["ysel"][0:N_EXP, 0:D][:, :], D_["gw"].ap()),
            (S["ebrow"][:, :], D_["eb"].ap().rearrange("(o e) -> o e", o=1)),
            (S["wcd"][:, :].rearrange("p (c l) -> p c l", l=LAT),
             D_["wcd"].ap().rearrange("(c p) l -> p c l", p=128)),
            (S["wq"][:, :].rearrange("p (c q) -> p c q", q=D),
             D_["wq"].ap().rearrange("(c p) q -> p c q", p=128)),
            (S["wkv"][:, :].rearrange("p (c k) -> p c k", k=2 * HD),
             D_["wkv"].ap().rearrange("(c p) k -> p c k", p=128)),
            (S["wproj"][:, :].rearrange("p (c o) -> p c o", o=D),
             D_["wproj"].ap().rearrange("(c p) o -> p c o", p=128)),
        ]
        for o, i in loads:
            self.dma("gpsimd", "mld", o, i)
        self.mld_all = ch["mld"].n
        for name, v in (("ones1f", 1.0), ("onesb", 1.0), ("onescol", 1.0),
                        ("ones8", 1.0), ("carry", 0.0),
                        ("epsv", 1e-5)):
            self.pool("memset", ap=S[name][:, :], constant=float(v))
        self.pool("memset", ap=S["v_aug"][:, :], constant=1.0)
        self.pool_init = ch["pool"].n
        mld_w = (ch["mld"], self.mld_all)
        pool_w = (ch["pool"], self.pool_init)

        bank, slot, bw = self.tbank()
        last = None
        for c in range(8):
            last = self.tr(bank[:, c * 8:(c + 1) * 8],
                           S["ysel"][0:N_EXP, 0:D][0:8, c * 128:(c + 1) * 128],
                           S["eyeb"][0:8, 0:8],
                           waits=([mld_w] + bw) if c == 0 else ())
        ev = self.dve("tensor_copy", waits=[(ch["pe"], last)],
                      out=S["gate_wT"][:, :], in_=bank[:, 0:64])
        self.tbank_release(slot, (ch["dve"], ev))
        self.gate_wT_ready = (ch["dve"], ev)

        bank, bw = self.fb("F0")
        m = self.mm(bank[0:128, 0:N_EXP], S["ones1f"][:, :], S["ebrow"][:, :],
                    True, True, waits=[mld_w, pool_w] + bw)
        a = self.acopy(S["bias_b"][:, :], bank[0:128, 0:N_EXP],
                       waits=[(ch["pe"], m)])
        self.fb_release("F0", (ch["act"], a))
        self.bias_b_ready = (ch["act"], a)
        self.conv_free = [None, None]
        self.ev_free = None
        self.conv_mark = {}

    def emit_conv_expert(self, e):
        S, D_, ch = self.sb, self.dram, self.ch
        chp, chd, chc, chs = ch["pe"], ch["dve"], ch["cvl"], ch["cst"]
        mld_w = (ch["mld"], self.mld_all)
        convs = [S["conv0"], S["conv1"]]
        for q in range(8):
            r = q % 2
            ld = self.dma("gpsimd", "cvl",
                          convs[r][:, :].rearrange("p (a h) -> p a h", h=1024),
                          D_["wup"].ap()[e, q * 512:(q + 1) * 512, :].rearrange(
                              "(a p) h -> p a h", p=128),
                          waits=[self.conv_free[r], mld_w])
            for a in range(4):
                s = q * 4 + a
                g, sl = s // 4, s % 4
                hf, s2 = sl // 2, s % 2
                bank, slot, bw = self.tbank()
                last = None
                for c in range(8):
                    last = self.tr(bank[:, c * 128:(c + 1) * 128],
                                   convs[r][:, a * 1024 + c * 128:a * 1024 + (c + 1) * 128],
                                   S["eyeb"][:, :],
                                   waits=([(chc, ld)] + bw) if c == 0 else ())
                evw = [(chp, last)]
                if sl == 0 and self.ev_free is not None:
                    evw.append(self.ev_free)
                out_ap = S["ev"][:, hf * 2048:(hf + 1) * 2048].rearrange(
                    "p (c i) -> p c i", i=256)[:, :, s2 * 128:(s2 + 1) * 128]
                in_ap = bank[:, :].rearrange("p (c i) -> p c i", i=128)
                evv = self.dve("tensor_copy", waits=evw, out=out_ap, in_=in_ap)
                self.tbank_release(slot, (chd, evv))
                if sl == 3:
                    st = self.dma("sync", "cst", D_["up_s"].ap()[e, g], S["ev"][:, :],
                                  waits=[(chd, evv)])
                    self.ev_free = (chs, st)
            self.conv_free[r] = (chp, last)
        self.conv_mark[(e, "up")] = (chs, chs.n)
        for s in range(8):
            r = s % 2
            ld = self.dma("gpsimd", "cvl", convs[r][:, :],
                          D_["wdn"].ap()[e, s * 128:(s + 1) * 128, :],
                          waits=[self.conv_free[r]])
            last_ev = None
            last_tr = None
            for g8 in range(4):
                bank, slot, bw = self.tbank()
                last_tr = None
                for k in range(8):
                    ic = g8 * 8 + k
                    last_tr = self.tr(bank[:, k * 128:(k + 1) * 128],
                                      convs[r][:, ic * 128:(ic + 1) * 128],
                                      S["eyeb"][:, :],
                                      waits=([(chc, ld)] + bw) if k == 0 else ())
                evw = [(chp, last_tr)]
                if g8 == 0 and self.ev_free is not None:
                    evw.append(self.ev_free)
                last_ev = self.dve("tensor_copy", waits=evw,
                                   out=S["ev"][:, g8 * 1024:(g8 + 1) * 1024],
                                   in_=bank[:, :])
                self.tbank_release(slot, (chd, last_ev))
            self.conv_free[r] = (chp, last_tr)
            st = self.dma("sync", "cst", D_["dn_s"].ap()[e, s], S["ev"][:, :],
                          waits=[(chd, last_ev)])
            self.ev_free = (chs, st)
        self.conv_mark[(e, "dn")] = (chs, chs.n)

    def emit_ln(self, g):
        S, D_, ch = self.sb, self.dram, self.ch
        chx, chd, cha, chp = ch["x"], ch["dve"], ch["act"], ch["pe"]
        ev = None
        for tt in range(4):
            r = 0
            xt = S["x0"]
            sq = S["sq0"]
            xtb = tt * D
            ld = self.dma("scalar", "x", xt[:, :],
                          D_["x"].ap()[g * GRP + tt * 128:g * GRP + (tt + 1) * 128, :],
                          waits=[self.marks.get(("x_free", r))])
            ldw = (chx, ld)
            cc = [r * 8 + k for k in range(8)]
            st = S["lnstat"]
            d1 = self.dve("reduce_sum", waits=[ldw], out=st[:, cc[0]:cc[0] + 1],
                          in_=xt[:, :], axis=AX.X)
            a1 = self.act(sq[:, :], xt[:, :], AF.Square,
                          waits=[ldw, self.marks.get(("sq_free", r))])
            d2 = self.dve("reduce_sum", waits=[(cha, a1)], out=st[:, cc[1]:cc[1] + 1],
                          in_=sq[:, :], axis=AX.X)
            self.marks[("sq_free", r)] = (chd, d2)
            d3 = self.dve("tensor_scalar", waits=[(chd, d1)],
                          out=st[:, cc[2]:cc[2] + 1], in0=st[:, cc[0]:cc[0] + 1],
                          scalar1=1.0 / D, scalar2=None, op0=ALU.mult)
            d4 = self.dve("tensor_scalar", waits=[(chd, d2)],
                          out=st[:, cc[3]:cc[3] + 1], in0=st[:, cc[1]:cc[1] + 1],
                          scalar1=1.0 / D, scalar2=None, op0=ALU.mult)
            d5 = self.dve("tensor_tensor", waits=[(chd, d3)],
                          out=st[:, cc[4]:cc[4] + 1], in0=st[:, cc[2]:cc[2] + 1],
                          in1=st[:, cc[2]:cc[2] + 1], op=ALU.mult)
            d6 = self.dve("tensor_tensor", waits=[(chd, d5)],
                          out=st[:, cc[5]:cc[5] + 1], in0=st[:, cc[3]:cc[3] + 1],
                          in1=st[:, cc[4]:cc[4] + 1], op=ALU.subtract)
            a2 = self.act(st[:, cc[6]:cc[6] + 1], st[:, cc[5]:cc[5] + 1], AF.Ln,
                          waits=[(chd, d6)], bias=S["epsv"][:, 0:1])
            a3 = self.act(st[:, cc[7]:cc[7] + 1], st[:, cc[6]:cc[6] + 1], AF.Exp,
                          waits=[(cha, a2)], scale=-0.5)
            d7 = self.dve("tensor_scalar",
                          waits=[(cha, a3), self.marks.get("xn_tok_free")],
                          out=S["xn_tok"][:, xtb:xtb + D],
                          in0=xt[:, :], scalar1=st[:, cc[2]:cc[2] + 1],
                          scalar2=st[:, cc[7]:cc[7] + 1], op0=ALU.subtract,
                          op1=ALU.mult)
            self.marks[("x_free", r)] = (chd, d7)
            bank, slot, bw = self.tbank()
            last = None
            for c in range(8):
                last = self.tr(bank[:, c * 128:(c + 1) * 128],
                               S["xn_tok"][:, xtb + c * 128:xtb + (c + 1) * 128],
                               S["eyeb"][:, :],
                               waits=([(chd, d7)] + bw) if c == 0 else ())
            ev = self.dve(
                "tensor_copy", waits=[(chp, last)],
                out=S["xn"][:, :].rearrange("p (c t) -> p c t", t=GRP)[:, :, tt * 128:(tt + 1) * 128],
                in_=bank[:, :].rearrange("p (c t) -> p c t", t=128))
            self.tbank_release(slot, (chd, ev))
        self.xn_ready = (chd, ev)

    def emit_latent(self, g):
        S, ch = self.sb, self.ch
        chp, cha = ch["pe"], ch["act"]
        names = ["F0", "F1"]
        a = None
        for l in range(4):
            bank, bw = self.fb(names[l % 2])
            m = None
            for c in range(8):
                m = self.mm(bank[:, :],
                            S["wcd"][:, c * 512 + l * 128:c * 512 + (l + 1) * 128],
                            S["xn"][:, c * GRP:(c + 1) * GRP], c == 0, c == 7,
                            waits=([self.xn_ready] + bw) if c == 0 else ())
            a = self.acopy(S["latent"][:, l * GRP:(l + 1) * GRP], bank[:, :],
                           waits=[(chp, m)])
            self.fb_release(names[l % 2], (cha, a))
        self.latent_ready = (cha, a)

    def emit_kv(self, g):
        S, ch = self.sb, self.ch
        chp, cha, chd = ch["pe"], ch["act"], ch["dve"]
        t0 = (g % 4) * GRP
        ow = self.marks.get(("attn_done", 3)) if g == 4 else None
        bank, bw = self.fb("F2")
        m = None
        for l in range(4):
            m = self.mm(bank[:, :], S["wkv"][:, l * 128:(l + 1) * 128],
                        S["latent"][:, l * GRP:(l + 1) * GRP], l == 0, l == 3,
                        waits=([self.latent_ready] + bw) if l == 0 else ())
        a1 = self.acopy(S["kvb"][:, :], bank[:, :],
                        waits=[(chp, m), self.marks.get("kvb_free")])
        self.fb_release("F2", (cha, a1))
        bank3, bw3 = self.fb("F3")
        m2 = self.mm(bank3[:, :], S["dup64"][:, :], S["kvb"][0:64, :], True, True,
                     waits=[(cha, a1)] + bw3)
        a2 = self.acopy(S["k_dup"][:, t0:t0 + GRP], bank3[:, :],
                        waits=[(chp, m2), ow])
        self.fb_release("F3", (cha, a2))
        bank, slot, tbw = self.tbank()
        last = None
        for jj in range(4):
            last = self.tr(bank[:, jj * 64:(jj + 1) * 64],
                           S["kvb"][64:128, jj * 128:(jj + 1) * 128],
                           S["eyeb"][64:128, 64:128],
                           waits=([(cha, a1)] + tbw) if jj == 0 else ())
        j0 = t0 // 128
        vw = [(chp, last), (self.ch["pool"], self.pool_init), ow]
        evv = self.dve("tensor_copy", waits=vw,
                       out=S["v_aug"][:, :].rearrange("p (j c) -> p j c", c=65)[:, j0:j0 + 4, 0:64],
                       in_=bank[:, 0:256].rearrange("p (j c) -> p j c", c=64))
        self.tbank_release(slot, (chd, evv))
        self.marks[("kv_ready", g % 4)] = (cha, a2)
        self.marks[("v_ready", g % 4)] = (chd, evv)
        self.marks["kvb_free"] = (chp, last)

    def emit_gate(self, g):
        S, ch = self.sb, self.ch
        chp, cha, chd = ch["pe"], ch["act"], ch["dve"]
        E = N_EXP
        W = S["tmp0"]
        for tt in range(4):
            b = tt * 40
            R = S["routg"]
            mask1 = R[:, b:b + 8]
            mask2 = R[:, b + 8:b + 16]
            m12 = R[:, b + 16:b + 24]
            wsel = R[:, b + 24:b + 32]
            w1c = R[:, b + 32:b + 33]
            w2c = R[:, b + 33:b + 34]
            bank, bw = self.fb("F3")
            m = None
            for c in range(8):
                m = self.mm(bank[0:128, 0:E],
                            S["xn"][:, c * GRP + tt * 128:c * GRP + (tt + 1) * 128],
                            S["gate_wT"][:, c * E:(c + 1) * E], c == 0, c == 7,
                            waits=([self.xn_ready, self.gate_wT_ready,
                                    self.bias_b_ready] + bw) if c == 0 else ())
            wg = W[:, 0:E]
            e_t = W[:, E:2 * E]
            w_ = W[:, 2 * E:3 * E]
            m1 = W[:, 3 * E:3 * E + 1]
            nm = W[:, 3 * E + 1:3 * E + 2]
            w2 = W[:, 5 * E:6 * E]
            m2 = W[:, 6 * E:6 * E + 1]
            dsum = W[:, 9 * E:9 * E + 1]
            rd = W[:, 9 * E + 1:9 * E + 2]
            Z = W[:, 9 * E + 2:9 * E + 3]
            rZ = W[:, 9 * E + 3:9 * E + 4]
            t8 = W[:, 11 * E:12 * E]
            d0 = self.dve("tensor_tensor",
                          waits=[(chp, m), self.marks.get(("tmp_free", 0)),
                                 self.marks.get(("routg_free", tt))],
                          out=wg, in0=bank[0:128, 0:E], in1=S["bias_b"][:, :],
                          op=ALU.add)
            self.fb_release("F3", (chd, d0))
            dm = self.dve("reduce_max", waits=[(chd, d0)], out=nm, in_=wg,
                          axis=AX.X, negate=True)
            a1 = self.act(e_t, wg, AF.Exp, waits=[(chd, dm)], bias=nm)
            d2 = self.dve("reduce_sum", waits=[(cha, a1)], out=Z, in_=e_t, axis=AX.X)
            d3 = self.dve("reciprocal", waits=[(chd, d2)], out=rZ, in_=Z)
            d4 = self.dve("tensor_scalar", waits=[(chd, d3)], out=w_, in0=e_t,
                          scalar1=rZ, scalar2=None, op0=ALU.mult)
            d5 = self.dve("reduce_max", waits=[(chd, d4)], out=m1, in_=w_, axis=AX.X)
            d6 = self.dve("tensor_scalar", waits=[(chd, d5)], out=mask1, in0=w_,
                          scalar1=m1, scalar2=None, op0=ALU.is_ge)
            d7 = self.dve("scalar_tensor_tensor", waits=[(chd, d6)], out=w2,
                          in0=mask1, scalar=-1e30, in1=w_, op0=ALU.mult, op1=ALU.add)
            d8 = self.dve("reduce_max", waits=[(chd, d7)], out=m2, in_=w2, axis=AX.X)
            d9 = self.dve("tensor_scalar", waits=[(chd, d8)], out=mask2, in0=w2,
                          scalar1=m2, scalar2=None, op0=ALU.is_ge)
            d10 = self.dve("tensor_tensor", waits=[(chd, d9)], out=m12, in0=mask1,
                           in1=mask2, op=ALU.add)
            d11 = self.dve("tensor_tensor", waits=[(chd, d8)], out=dsum, in0=m1,
                           in1=m2, op=ALU.add)
            d12 = self.dve("tensor_scalar", waits=[(chd, d11)], out=dsum, in0=dsum,
                           scalar1=1e-6, scalar2=None, op0=ALU.add)
            d13 = self.dve("reciprocal", waits=[(chd, d12)], out=rd, in_=dsum)
            d14 = self.dve("tensor_tensor", waits=[(chd, d13)], out=w1c, in0=m1,
                           in1=rd, op=ALU.mult)
            d15 = self.dve("tensor_tensor", waits=[(chd, d14)], out=w2c, in0=m2,
                           in1=rd, op=ALU.mult)
            d16 = self.dve("tensor_scalar", waits=[(chd, d15)], out=t8, in0=mask1,
                           scalar1=w1c, scalar2=None, op0=ALU.mult)
            d17 = self.dve("tensor_scalar", waits=[(chd, d16)], out=wsel, in0=mask2,
                           scalar1=w2c, scalar2=None, op0=ALU.mult)
            d18 = self.dve("tensor_tensor", waits=[(chd, d17)], out=wsel, in0=wsel,
                           in1=t8, op=ALU.add)
            self.marks[("tmp_free", 0)] = (chd, d18)
            self.marks[("gatechain", tt)] = (chd, d18)
        bank, slot, bw = self.tbank()
        b32 = bank[:, :].bitcast(F32)
        last = None
        for tt in range(4):
            last = self.tr(b32[0:8, tt * 128:(tt + 1) * 128],
                           S["routg"][:, tt * 40 + 16:tt * 40 + 24], S["eyef"][:, :],
                           waits=([self.marks[("gatechain", 3)],
                                   self.marks.get("mrow_free")] + bw) if tt == 0 else ())
        am = self.acopy(S["mrow"][:, :], b32[0:8, 0:512], waits=[(chp, last)])
        self.tbank_release(slot, (cha, am))
        s1 = self.dve("tensor_tensor_scan",
                      waits=[(cha, am), self.marks.get("carry_free")],
                      out=S["posrow"][:, :], data0=S["ones8"][:, :],
                      data1=S["mrow"][:, :],
                      initial=0.0, op0=ALU.mult, op1=ALU.add)
        s2 = self.dve("tensor_copy", waits=[(chd, s1)], out=S["carry"][:, :],
                      in_=S["posrow"][:, 511:512])
        self.marks["carry_free"] = (chd, s2)
        s3 = self.dve("tensor_tensor", waits=[(chd, s2)], out=S["mrow"][:, :],
                      in0=S["posrow"][:, :], in1=S["mrow"][:, :], op=ALU.subtract)
        mr_last = None
        for tt in range(4):
            bank, slot, bw = self.tbank()
            b32 = bank[:, :].bitcast(F32)
            trp = self.tr(b32[:, 0:8], S["mrow"][0:8, tt * 128:(tt + 1) * 128],
                          S["eyef"][0:8, 0:8],
                          waits=[(chd, s3), self.marks.get("ptok_free")] + bw)
            ap = self.acopy(S["ptok"][:, tt * 8:(tt + 1) * 8], b32[:, 0:8],
                            waits=[(chp, trp)])
            self.tbank_release(slot, (cha, ap))
            mr_last = trp
        self.marks["mrow_free"] = (chp, mr_last)
        self.ptok_ready = (cha, ap)

    def tbank32(self, bank):
        return bank[:, :].bitcast(F32)[:, 0:512]

    def build_attention_units(self, g):
        S, ch = self.sb, self.ch
        chp, cha, chd = ch["pe"], ch["act"], ch["dve"]
        t0 = (g % 4) * GRP
        J = t0 // 128 + 4
        diag0 = t0 // 128
        kvw = [self.marks[("kv_ready", q)] for q in range(g % 4 + 1)]
        vw = [self.marks[("v_ready", q)] for q in range(g % 4 + 1)]
        units = []
        st = {"y_marks": []}

        def q_unit(p):
            def fn():
                qb = S["q0"]
                bank, slot, bw = self.tbank()
                b32 = self.tbank32(bank)
                m = None
                for l in range(4):
                    m = self.mm(b32, S["wq"][:, l * D + p * 128:l * D + (p + 1) * 128],
                                S["latent"][:, l * GRP:(l + 1) * GRP], l == 0, l == 3,
                                waits=([self.latent_ready,
                                        self.marks.get(("q_free", 0))] + bw)
                                if l == 0 else ())
                aq = self.acopy(qb[:, :], b32, waits=[(chp, m)])
                self.tbank_release(slot, (cha, aq))
                st[("qw", p)] = (cha, aq)
            return fn

        def score_unit(p, j):
            def fn():
                qb = S["q0"]
                pT = S["pT0"] if j % 2 == 0 else S["pT1"]
                bw = self.tbank_pair()
                TBf = self.ps["TB"][:, :].bitcast(F32)
                bA = TBf[:, 0:512]
                bB = TBf[:, 512:1024]
                mA = self.mm(bA, S["k_dup"][0:64, j * 128:(j + 1) * 128],
                             qb[0:64, :], True, True,
                             waits=[st[("qw", p)]] + kvw + bw)
                mB = self.mm(bB, S["k_dup"][64:128, j * 128:(j + 1) * 128],
                             qb[64:128, :], True, True)
                expw = [(chp, mB)]
                if j >= diag0:
                    o = (j - diag0) * 128
                    msl = S["maskc"][:, 384 - o:896 - o]
                    dA = self.dve("tensor_tensor", waits=[(chp, mA)], out=bA,
                                  in0=bA, in1=msl, op=ALU.add)
                    dB = self.dve("tensor_tensor", waits=[(chp, mB)], out=bB,
                                  in0=bB, in1=msl, op=ALU.add)
                    expw = [(chd, dB)]
                pfree = self.marks.get(("pT_free", j % 2))
                eA = self.act(pT[:, 0:1024], TBf[:, 0:1024], AF.Exp,
                              waits=expw + [pfree], scale=0.125)
                self.tbank_release(0, (cha, eA))
                self.tbank_release(1, (cha, eA))
                st[("exp", p, j)] = [(cha, eA)]
            return fn

        def av_unit(p, j):
            def fn():
                jr = j % 2
                pT = S["pT0"] if jr == 0 else S["pT1"]
                avw = list(st.pop(("exp", p, j)))
                if j == 0:
                    _, bw2 = self.fb("F4")
                    _, bw3 = self.fb("F5")
                    avw = avw + vw + bw2 + bw3
                m1_ = self.mm(self.ps["F4"][0:65, :],
                              S["v_aug"][:, j * 65:j * 65 + 65], pT[:, 0:512],
                              j == 0, j == J - 1, waits=avw)
                m2_ = self.mm(self.ps["F5"][64:128, :],
                              S["v_aug"][:, j * 65:j * 65 + 64], pT[:, 512:1024],
                              j == 0, j == J - 1, tile_position=(0, 64))
                m3 = self.mm(self.ps["F5"][0:1, :], S["onescol"][:, :],
                             pT[:, 512:1024], j == 0, j == J - 1)
                self.marks[("pT_free", jr)] = (chp, m3)
                st[("av", p)] = (chp, m3)
            return fn

        def norm_a_unit(p):
            def fn():
                m3 = st[("av", p)]
                self.marks[("q_free", 0)] = m3
                d1 = self.dve("reciprocal", waits=[m3, self.marks.get("rz_free")],
                              allow_lp=True,
                              out=S["rz"][64:65, 0:512], in_=self.ps["F4"][64:65, :])
                d2 = self.dve("reciprocal", waits=[m3], allow_lp=True,
                              out=S["rz"][0:1, 0:512], in_=self.ps["F5"][0:1, :])
                st[("rec", p)] = (d1, d2)
            return fn

        def norm_b_unit(p):
            def fn():
                d1, d2 = st.pop(("rec", p))
                bkA, slA, bwA = self.tbank()
                zA = self.tbank32(bkA)
                bkB, slB, bwB = self.tbank()
                zB = self.tbank32(bkB)
                mzA = self.mm(zA[0:64, :], S["onesb"][64:65, 0:64],
                              S["rz"][64:65, 0:512], True, True,
                              waits=[(chd, d1)] + bwA)
                mzB = self.mm(zB[64:128, :], S["onesb"][0:1, 0:64],
                              S["rz"][0:1, 0:512], True, True,
                              waits=[(chd, d2)] + bwB, tile_position=(0, 64))
                self.marks["rz_free"] = (chp, mzB)
                azA = self.acopy(S["zb"][0:64, :], zA[0:64, :],
                                 waits=[(chp, mzA), self.marks.get("zb_free")])
                azB = self.acopy(S["zb"][64:128, :], zB[64:128, :],
                                 waits=[(chp, mzB)])
                self.tbank_release(slA, (cha, azA))
                self.tbank_release(slB, (cha, azB))
                dyA = self.dve("tensor_tensor", waits=[(cha, azA)],
                               out=S["y_g"][0:64, p * GRP:(p + 1) * GRP],
                               in0=self.ps["F4"][0:64, :], in1=S["zb"][0:64, :],
                               op=ALU.mult)
                dyB = self.dve("tensor_tensor", waits=[(cha, azB)],
                               out=S["y_g"][64:128, p * GRP:(p + 1) * GRP],
                               in0=self.ps["F5"][64:128, :], in1=S["zb"][64:128, :],
                               op=ALU.mult)
                self.marks["zb_free"] = (chd, dyB)
                self.fb_release("F4", (chd, dyA))
                self.fb_release("F5", (chd, dyB))
                st["y_marks"] += [(chd, dyA), (chd, dyB)]
                if g % 4 == 3 and p == N_PAIR - 1:
                    self.marks[("attn_done", 3)] = st[("av", p)]
            return fn

        for p in range(N_PAIR):
            units.append(q_unit(p))
            for j in range(J):
                units.append(score_unit(p, j))
                units.append(av_unit(p, j))
            units.append(norm_a_unit(p))
            units.append(norm_b_unit(p))
        self._attn_units = units
        self._attn_state = st
        self._slots_left = N_EXP * 24
        return units

    def pop_unit(self):
        n = self._slots_left
        self._slots_left = max(1, n - 1)
        import math
        k = max(1, math.ceil(len(self._attn_units) / max(1, n)))
        for _ in range(min(k, len(self._attn_units))):
            self._attn_units.pop(0)()

    def emit_proj(self, g):
        S, ch = self.sb, self.ch
        chp, cha = ch["pe"], ch["act"]
        y_marks = self._attn_state["y_marks"]
        names = ["F4", "F5"]
        attn_marks = []
        for ho in range(8):
            bank, bw = self.fb(names[ho % 2])
            m = None
            for c in range(8):
                m = self.mm(bank[:, :],
                            S["wproj"][:, c * D + ho * 128:c * D + (ho + 1) * 128],
                            S["y_g"][:, c * GRP:(c + 1) * GRP], c == 0, c == 7,
                            waits=(y_marks + bw) if c == 0 else ())
            a = self.acopy(S["attn"][:, ho * GRP:(ho + 1) * GRP], bank[:, :],
                           waits=[(chp, m)])
            self.fb_release(names[ho % 2], (cha, a))
            attn_marks.append((cha, a))
        self.attn_ready = attn_marks

    def emit_ffn(self, g, e):
        S, D_, ch = self.sb, self.dram, self.ch
        chp, cha, chd = ch["pe"], ch["act"], ch["dve"]
        chu, chn = ch["wup"], ch["wdn"]
        h = S["arena"]
        conv_u = self.conv_mark[(e, "up")]
        conv_d = self.conv_mark[(e, "dn")]
        up_bufs = [S["up_buf0"], S["up_buf1"], S["up_buf2"], S["up_buf3"]]
        dn_bufs = [S["dn_buf0"], S["dn_buf1"], S["dn_buf2"], S["dn_buf3"]]

        def load_up(k):
            gg, hf = k // 2, k % 2
            ld = self.dma("gpsimd", "wup", up_bufs[k % 4][:, :],
                          D_["up_s"].ap()[e, gg][:, hf * 2048:(hf + 1) * 2048],
                          waits=[self.marks.get(("upb_free", k % 4)), conv_u])
            self.marks[("upb_loaded", k)] = (chu, ld)

        def load_dn(k):
            t, hf = k // 2, k % 2
            ld = self.dma("sync", "wdn", dn_bufs[k % 4][:, :],
                          D_["dn_s"].ap()[e, t][:, hf * 2048:(hf + 1) * 2048],
                          waits=[self.marks.get(("dnb_free", k % 4)), conv_d])
            self.marks[("dnb_loaded", k)] = (chn, ld)

        for k in range(4):
            load_up(k)
        W = S["tmp0"]
        p_marks = []
        pw_marks = []
        for tt in range(4):
            b = tt * 40
            qcol = W[:, 14 * N_EXP:14 * N_EXP + 1]
            q1 = self.dve("tensor_scalar",
                          waits=[self.ptok_ready, self.marks.get(("tmp_free", 0)),
                                 self.marks.get(("Pg_free", tt))],
                          out=qcol, in0=S["ptok"][:, tt * 8 + e:tt * 8 + e + 1],
                          scalar1=4096.0, scalar2=None, op0=ALU.add)
            q2 = self.dve("scalar_tensor_tensor", waits=[(chd, q1)], out=qcol,
                          in0=S["routg"][:, b + 16 + e:b + 17 + e], scalar=-4096.0,
                          in1=qcol, op0=ALU.mult, op1=ALU.add)
            q3 = self.dve("tensor_scalar", waits=[(chd, q2)],
                          out=S["Pg"][:, tt * CSEL:(tt + 1) * CSEL],
                          in0=S["jrow"][:, :], scalar1=qcol, scalar2=None,
                          op0=ALU.is_equal)
            q4 = self.dve("tensor_scalar", waits=[(chd, q3)],
                          out=S["Pw"][:, tt * CSEL:(tt + 1) * CSEL],
                          in0=S["Pg"][:, tt * CSEL:(tt + 1) * CSEL],
                          scalar1=S["routg"][:, b + 24 + e:b + 25 + e],
                          scalar2=None, op0=ALU.mult)
            self.marks[("tmp_free", 0)] = (chd, q2)
            p_marks.append((chd, q3))
            pw_marks.append((chd, q4))
        pbanks = ["F0", "F1"]
        gxn_marks = []
        for dch in range(8):
            bank, bw = self.fb(pbanks[dch % 2])
            m = None
            for tt in range(4):
                w = ()
                if tt == 0:
                    w = p_marks + bw + [self.xn_ready]
                m = self.mm(bank[:, 0:CSEL],
                            S["xn_tok"][:, tt * D + dch * 128:tt * D + (dch + 1) * 128],
                            S["Pg"][:, tt * CSEL:(tt + 1) * CSEL],
                            tt == 0, tt == 3, waits=w)
            a = self.dve("tensor_copy", waits=[(chp, m)],
                         out=S["gxn"][:, dch * CSEL:(dch + 1) * CSEL],
                         in_=bank[:, 0:CSEL])
            self.fb_release(pbanks[dch % 2], (chd, a))
            gxn_marks.append((chd, a))
        self.marks["xn_tok_used"] = (chp, m)
        ptw_marks = []
        for jh in range(2):
            bank, slot, bw = self.tbank()
            last = None
            for tt in range(4):
                last = self.tr(bank[:, tt * 128:(tt + 1) * 128],
                               S["Pw"][:, tt * CSEL + jh * 128:tt * CSEL + (jh + 1) * 128],
                               S["eyeb"][:, :],
                               waits=(pw_marks + bw +
                                      [self.marks.get("PTw_free")]) if tt == 0 else ())
            ev = self.dve("tensor_copy", waits=[(chp, last)],
                          out=S["PTw"][:, jh * GRP:(jh + 1) * GRP],
                          in_=bank[:, 0:512])
            self.tbank_release(slot, (chd, ev))
            ptw_marks.append((chd, ev))
            self.marks[("Pg_free", 10 + jh)] = (chp, last)
        for tt in range(4):
            self.marks[("Pg_free", tt)] = (chp, last)
        ubanks = ["F0", "F1"]
        gelu_last = None
        for k in range(16):
            for it2 in range(2):
                itile = k * 2 + it2
                bname = ubanks[itile % 2]
                bank, bw = self.fb(bname)
                m = None
                for c in range(8):
                    w = ()
                    if c == 0:
                        w = [self.marks[("upb_loaded", k)]] + gxn_marks + bw
                    m = self.mm(bank[:, 0:CSEL],
                                up_bufs[k % 4][:, c * 256 + it2 * 128:
                                               c * 256 + it2 * 128 + 128],
                                S["gxn"][:, c * CSEL:(c + 1) * CSEL],
                                c == 0, c == 7, waits=w)
            a = self.act(
                h[:, (k * 2) * CSEL:(k * 2 + 2) * CSEL].rearrange(
                    "p (b j) -> p b j", j=CSEL),
                self.ps["FU"][:, :].rearrange("p (b j) -> p b j", j=512)[:, :, 0:CSEL],
                AF.Gelu, waits=[(chp, m)])
            self.fb_release("F0", (cha, a))
            self.fb_release("F1", (cha, a))
            gelu_last = a
            self.marks[("upb_free", k % 4)] = (chp, m)
            if k + 4 < 16:
                load_up(k + 4)
            self.pop_unit()
        for k in range(4):
            load_dn(k)
        dbanks = ["F2", "F3"]
        ysel_marks = []
        for t in range(8):
            bname = dbanks[t % 2]
            bank, bw = self.fb(bname)
            m = None
            for ic in range(32):
                w = ()
                if ic == 0:
                    w = [self.marks[("dnb_loaded", 2 * t)], (cha, gelu_last),
                         self.marks.get("ysel_free")] + bw
                elif ic == 16:
                    w = [self.marks[("dnb_loaded", 2 * t + 1)]]
                m = self.mm(bank[:, 0:CSEL],
                            dn_bufs[(2 * t + ic // 16) % 4][:, (ic % 16) * 128:(ic % 16 + 1) * 128],
                            h[:, ic * CSEL:(ic + 1) * CSEL],
                            ic == 0, ic == 31, waits=w)
            a = self.dve("tensor_copy", waits=[(chp, m)],
                         out=S["ysel"][:, t * CSEL:(t + 1) * CSEL],
                         in_=bank[:, 0:CSEL])
            ysel_marks.append((chd, a))
            self.fb_release(bname, (chd, a))
            self.marks[("dnb_free", (2 * t) % 4)] = (chp, m)
            self.marks[("dnb_free", (2 * t + 1) % 4)] = (chp, m)
            if 2 * t + 4 < 16:
                load_dn(2 * t + 4)
                load_dn(2 * t + 5)
            self.pop_unit()
        yst_marks = []
        for jh in range(2):
            bank, slot, bw = self.tbank()
            last = None
            for t in range(8):
                last = self.tr(bank[:, t * 128:(t + 1) * 128],
                               S["ysel"][:, t * CSEL + jh * 128:t * CSEL + (jh + 1) * 128],
                               S["eyeb"][:, :],
                               waits=(ysel_marks + bw +
                                      [self.marks.get("yselT_free")]) if t == 0 else ())
            ev = self.dve("tensor_copy", waits=[(chp, last)],
                          out=S["yselT"][:, jh * D:(jh + 1) * D], in_=bank[:, :])
            self.tbank_release(slot, (chd, ev))
            yst_marks.append((chd, ev))
        self.marks["ysel_free"] = (chp, last)
        rbanks = ["F2", "F3"]
        ridx = 0
        for tt in range(4):
            for dh in range(2):
                bname = rbanks[ridx % 2]
                ridx += 1
                bank, bw = self.fb(bname)
                m = None
                for jh in range(2):
                    w = ()
                    if jh == 0:
                        w = ptw_marks + yst_marks + bw
                    m = self.mm(bank[:, :],
                                S["PTw"][:, jh * GRP + tt * 128:jh * GRP + (tt + 1) * 128],
                                S["yselT"][:, jh * D + dh * 512:jh * D + (dh + 1) * 512],
                                jh == 0, jh == 1, waits=w)
                ochunk = S["out_ffn"][:, tt * D + dh * 512:tt * D + (dh + 1) * 512]
                if e == 0:
                    dA = self.dve("tensor_copy", waits=[(chp, m)], out=ochunk,
                                  in_=bank[:, :])
                else:
                    dA = self.dve("tensor_tensor",
                                  waits=[(chp, m), self.marks.get(("offn", tt, dh))],
                                  out=ochunk, in0=bank[:, :], in1=ochunk, op=ALU.add)
                self.fb_release(bname, (chd, dA))
                self.marks[("offn", tt, dh)] = (chd, dA)
        self.marks["PTw_free"] = (chp, m)
        self.marks["yselT_free"] = (chp, m)
        if e == N_EXP - 1:
            self.marks["xn_tok_free"] = (chp, m)

    def emit_combine(self, g):
        S, D_, ch = self.sb, self.dram, self.ch
        chp, cha, chd, chpl = ch["pe"], ch["act"], ch["dve"], ch["pool"]
        cho, chx = ch["out"], ch["x"]
        for tt in range(4):
            offw = [self.marks[("offn", tt, 0)], self.marks[("offn", tt, 1)]]
            ot = S["out_ffn"][:, tt * D:(tt + 1) * D]
            lx = self.dma("scalar", "x", S["x0"][:, :],
                          D_["x"].ap()[g * GRP + tt * 128:g * GRP + (tt + 1) * 128, :],
                          waits=[self.marks.get(("x_free", 0))])
            st = S["lnstat"]
            d1 = self.dve("reduce_sum", waits=offw, out=st[:, 0:1], in_=ot,
                          axis=AX.X)
            a1 = self.act(S["sq0"][:, :], ot, AF.Square,
                          waits=offw + [self.marks.get(("sq_free", 0))])
            d2 = self.dve("reduce_sum", waits=[(cha, a1)], out=st[:, 1:2],
                          in_=S["sq0"][:, :], axis=AX.X)
            self.marks[("sq_free", 0)] = (chd, d2)
            d3 = self.dve("tensor_scalar", waits=[(chd, d1)], out=st[:, 2:3],
                          in0=st[:, 0:1], scalar1=1.0 / D, scalar2=None,
                          op0=ALU.mult)
            d4 = self.dve("tensor_scalar", waits=[(chd, d2)], out=st[:, 3:4],
                          in0=st[:, 1:2], scalar1=1.0 / D, scalar2=None,
                          op0=ALU.mult)
            d5 = self.dve("tensor_tensor", waits=[(chd, d3)], out=st[:, 4:5],
                          in0=st[:, 2:3], in1=st[:, 2:3], op=ALU.mult)
            d6 = self.dve("tensor_tensor", waits=[(chd, d5)], out=st[:, 5:6],
                          in0=st[:, 3:4], in1=st[:, 4:5], op=ALU.subtract)
            a2 = self.act(st[:, 6:7], st[:, 5:6], AF.Ln, waits=[(chd, d6)],
                          bias=S["epsv"][:, 0:1])
            a3 = self.act(st[:, 7:8], st[:, 6:7], AF.Exp, waits=[(cha, a2)],
                          scale=-0.5)
            d8 = self.dve("tensor_scalar", waits=[(cha, a3)], out=ot, in0=ot,
                          scalar1=st[:, 2:3], scalar2=st[:, 7:8],
                          op0=ALU.subtract, op1=ALU.mult)
            p1 = self.pool("tensor_tensor", waits=[(chd, d8), (chx, lx)],
                           out=ot, in0=ot, in1=S["x0"][:, :], op=ALU.add)
            self.marks[("x_free", 0)] = (chpl, p1)
            d9 = self.dve("tensor_tensor", waits=[(chpl, p1)], out=ot, in0=ot,
                          in1=S["xn_tok"][:, tt * D:(tt + 1) * D], op=ALU.add)
            bank, slot, bw = self.tbank()
            last = None
            for c in range(8):
                last = self.tr(bank[:, c * 128:(c + 1) * 128],
                               S["attn"][:, c * GRP + tt * 128:c * GRP + (tt + 1) * 128],
                               S["eyeb"][:, :],
                               waits=(self.attn_ready + bw) if c == 0 else ())
            d10 = self.dve("tensor_tensor", waits=[(chp, last), (chd, d9)],
                           out=ot, in0=bank[:, :], in1=ot, op=ALU.add)
            self.tbank_release(slot, (chd, d10))
            stm = self.dma("sync", "out",
                           D_["out"].ap()[g * GRP + tt * 128:g * GRP + (tt + 1) * 128, :],
                           ot, waits=[(chd, d10)])
            self.marks[("offn", tt, 0)] = (cho, stm)
            self.marks[("offn", tt, 1)] = (cho, stm)

    def emit_all(self):
        self.emit_prologue()
        for e in range(4):
            self.emit_conv_expert(e)
        for g in range(N_GRP):
            self.emit_ln(g)
            self.emit_latent(g)
            self.emit_kv(g)
            self.emit_gate(g)
            self.build_attention_units(g)
            for e in range(N_EXP):
                self.emit_ffn(g, e)
                if g == 0 and e < 4:
                    self.emit_conv_expert(e + 4)
            while self._attn_units:
                self.pop_unit()
            self.emit_proj(g)
            self.emit_combine(g)
        self.barrier("sync", [(self.ch["out"], self.ch["out"].n)])

    def finalize(self):
        nc, P, sems = self.nc, self.P, self.sems
        with nc.Block() as block:
            def make(engname):
                ops = [(fn, waits, incs) for (e, fn, waits, incs) in P.ops
                       if e == engname]

                def section(eng, ops=ops):
                    for fn, waits, incs in ops:
                        for chn, v in waits:
                            eng.wait_ge(sems[chn.name], v)
                        if fn is None:
                            continue
                        instr = fn(eng)
                        for chn, k in incs:
                            instr.then_inc(sems[chn.name], k)
                return section
            block.tensor(make("tensor"))
            block.scalar(make("scalar"))
            block.vector(make("vector"))
            block.gpsimd(make("gpsimd"))
            block.sync(make("sync"))


def build_program():
    b = Builder()
    b.declare()
    with contextlib.ExitStack() as es:
        b.alloc(es)
        b.emit_all()
        b.P.check_deadlock()
        b.finalize()
    return b.nc


def _consts():
    import ml_dtypes
    bf16 = ml_dtypes.bfloat16
    eyeb = np.eye(128, dtype=bf16)
    eyef = np.eye(128, dtype=np.float32)
    ii = np.arange(128)[:, None]
    uu = np.arange(896)[None, :]
    maskc = np.where(uu < ii + 384, np.float32(-1e30), np.float32(0.0)).astype(bf16)
    dup64 = np.concatenate([np.eye(64), np.eye(64)], axis=1).astype(bf16)
    jrow = np.broadcast_to(np.arange(CSEL, dtype=np.float32), (128, CSEL)).copy()
    return eyeb, eyef, maskc, dup64, jrow


def kernel(L0_attn_x, L0_attn_Wc_down, L0_attn_Wq_up, L0_attn_Wkv_up,
           L0_attn_Wc_proj, L0_ffn_gate_w, L0_ffn_expert_bias, L0_ffn_w_up,
           L0_ffn_w_down):
    x = np.ascontiguousarray(np.asarray(L0_attn_x, dtype=np.float32))
    wcd = np.ascontiguousarray(np.asarray(L0_attn_Wc_down, dtype=np.float32))
    wq = np.ascontiguousarray(np.asarray(L0_attn_Wq_up, dtype=np.float32))
    wkv = np.ascontiguousarray(np.asarray(L0_attn_Wkv_up, dtype=np.float32))
    wproj = np.ascontiguousarray(np.asarray(L0_attn_Wc_proj, dtype=np.float32))
    gw = np.ascontiguousarray(np.asarray(L0_ffn_gate_w, dtype=np.float32))
    eb = np.ascontiguousarray(np.asarray(L0_ffn_expert_bias, dtype=np.float32))
    wup = np.ascontiguousarray(np.asarray(L0_ffn_w_up, dtype=np.float32))
    wdn = np.ascontiguousarray(np.asarray(L0_ffn_w_down, dtype=np.float32))
    eyeb, eyef, maskc, dup64, jrow = _consts()
    nc = build_program()
    in_maps = []
    for c in range(N_CORES):
        in_maps.append({
            "x": np.ascontiguousarray(x[2 * c:2 * c + 2].reshape(N_TOK, D)),
            "d_wcd": wcd, "d_wq": wq, "d_wkv": wkv, "d_wproj": wproj,
            "gw": gw, "eb": eb, "wup": wup, "wdn": wdn,
            "d_eyeb": eyeb, "d_eyef": eyef, "d_maskc": maskc, "d_dup64": dup64,
            "d_jrow": jrow,
        })
    res = run_bass_kernel_spmd(nc, in_maps, core_ids=list(range(N_CORES)))
    out = np.stack([r["out"] for r in res.results], axis=0)
    return out.reshape(16, 2048, 1024)


# version 15
# speedup vs baseline: 1.0746x; 1.0746x over previous
import contextlib
import numpy as np

import concourse.bass as bass
import concourse.mybir as mybir
from concourse.bass_utils import run_bass_kernel_spmd

F32 = mybir.dt.float32
BF16 = mybir.dt.bfloat16
AF = mybir.ActivationFunctionType
ALU = mybir.AluOpType
AX = mybir.AxisListType

N_CORES = 8
D = 1024
T_SEQ = 2048
N_TOK = 4096
GRP = 512
N_GRP = 8
N_EXP = 8
I_FFN = 4096
LAT = 512
HD = 64
N_PAIR = 8
CSEL = 256
HUGE = 16777216.0


class Ch:
    __slots__ = ("name", "n")

    def __init__(self, name):
        self.name = name
        self.n = 0


class Prog:
    def __init__(self):
        self.ops = []

    def op(self, eng, fn, waits=(), incs=()):
        wd = {}
        for w in waits:
            if w is None:
                continue
            ch, v = w
            if v > 0 and (ch not in wd or wd[ch] < v):
                wd[ch] = v
        rec = []
        for ch, k in incs:
            ch.n += k
            rec.append((ch, k))
        self.ops.append((eng, fn, tuple(wd.items()), tuple(rec)))

    def check_deadlock(self):
        by_eng = {e: [] for e in ("tensor", "scalar", "vector", "gpsimd", "sync")}
        for i, (eng, fn, waits, incs) in enumerate(self.ops):
            by_eng[eng].append((i, waits, incs))
        ptr = {e: 0 for e in by_eng}
        val = {}
        total = sum(len(v) for v in by_eng.values())
        done = 0
        progress = True
        while progress:
            progress = False
            for e, lst in by_eng.items():
                while ptr[e] < len(lst):
                    i, waits, incs = lst[ptr[e]]
                    if any(val.get(ch, 0) < v for ch, v in waits):
                        break
                    for ch, k in incs:
                        val[ch] = val.get(ch, 0) + k
                    ptr[e] += 1
                    done += 1
                    progress = True
        if done != total:
            msgs = []
            for e, lst in by_eng.items():
                if ptr[e] < len(lst):
                    i, waits, incs = lst[ptr[e]]
                    missing = [(ch.name, v, val.get(ch, 0)) for ch, v in waits
                               if val.get(ch, 0) < v]
                    msgs.append(f"{e}@op{i} needs {missing}")
            raise RuntimeError("DEADLOCK: " + "; ".join(msgs))


class Builder:
    def __init__(self):
        self.nc = bass.Bass("TRN2", target_bir_lowering=False, debug=False)
        self.P = Prog()
        self.ch = {n: Ch(n) for n in
                   ("pe", "act", "dve", "pool", "mld", "cvl", "cst", "wup",
                    "wdn", "x", "out")}
        self.sb = {}
        self.ps = {}
        self.dram = {}
        self.marks = {}
        self.tring_free = [None, None]
        self.tring_i = 0
        self.bank_free = {k: None for k in ("F0", "F1", "F2", "F3", "F4", "F5")}
        self._pm = {}

    def declare(self):
        nc, dr = self.nc, self.dram
        dr["x"] = nc.dram_tensor("x", [N_TOK, D], F32, kind="ExternalInput")
        dr["wcd"] = nc.dram_tensor("d_wcd", [D, LAT], F32, kind="ExternalInput")
        dr["wq"] = nc.dram_tensor("d_wq", [LAT, D], F32, kind="ExternalInput")
        dr["wkv"] = nc.dram_tensor("d_wkv", [LAT, 2 * HD], F32, kind="ExternalInput")
        dr["wproj"] = nc.dram_tensor("d_wproj", [D, D], F32, kind="ExternalInput")
        dr["gw"] = nc.dram_tensor("gw", [N_EXP, D], F32, kind="ExternalInput")
        dr["eb"] = nc.dram_tensor("eb", [N_EXP], F32, kind="ExternalInput")
        dr["wup"] = nc.dram_tensor("wup", [N_EXP, I_FFN, D], F32, kind="ExternalInput")
        dr["wdn"] = nc.dram_tensor("wdn", [N_EXP, D, I_FFN], F32, kind="ExternalInput")
        dr["eyeb"] = nc.dram_tensor("d_eyeb", [128, 128], BF16, kind="ExternalInput")
        dr["eyef"] = nc.dram_tensor("d_eyef", [128, 128], F32, kind="ExternalInput")
        dr["maskc"] = nc.dram_tensor("d_maskc", [128, 896], BF16, kind="ExternalInput")
        dr["dup64"] = nc.dram_tensor("d_dup64", [64, 128], BF16, kind="ExternalInput")
        dr["jrow"] = nc.dram_tensor("d_jrow", [128, CSEL], F32,
                                    kind="ExternalInput")
        dr["out"] = nc.dram_tensor("out", [N_TOK, D], F32, kind="ExternalOutput")
        dr["up_s"] = nc.dram_tensor("up_s", [N_EXP, 8, 128, 4096], BF16)
        dr["dn_s"] = nc.dram_tensor("dn_s", [N_EXP, 8, 128, 4096], BF16)

    def alloc(self, es):
        nc, S = self.nc, self.sb

        def sbuf(name, shape, dt):
            S[name] = es.enter_context(nc.sbuf_tensor(name, shape, dt))

        sbuf("eyeb", [128, 128], BF16)
        sbuf("eyef", [128, 128], F32)
        sbuf("maskc", [128, 896], BF16)
        sbuf("dup64", [64, 128], BF16)
        sbuf("ones1f", [1, 128], F32)
        sbuf("onesb", [128, 64], BF16)
        sbuf("onescol", [128, 1], BF16)
        sbuf("gate_wT", [128, 64], BF16)
        sbuf("ebrow", [1, N_EXP], F32)
        sbuf("bias_b", [128, N_EXP], F32)
        sbuf("wcd", [128, 4096], BF16)
        sbuf("wq", [128, 4096], BF16)
        sbuf("wkv", [128, 512], BF16)
        sbuf("wproj", [128, 8192], BF16)
        sbuf("k_dup", [128, T_SEQ], BF16)
        sbuf("v_aug", [128, 16 * 65], BF16)
        sbuf("xn", [128, 8 * GRP], BF16)
        sbuf("latent", [128, 4 * GRP], BF16)
        sbuf("routg", [128, 160], F32)
        sbuf("mrow", [8, GRP], F32)
        sbuf("posrow", [8, GRP], F32)
        sbuf("ones8", [8, GRP], F32)
        sbuf("carry", [8, 1], F32)
        sbuf("ptok", [128, 32], F32)
        sbuf("jrow", [128, CSEL], F32)
        sbuf("xn_tok", [128, 4 * D], BF16)
        sbuf("Pg", [128, 4 * CSEL], BF16)
        sbuf("Pw", [128, 4 * CSEL], BF16)
        sbuf("PTw", [128, 2 * GRP], BF16)
        sbuf("gxn", [128, 8 * CSEL], BF16)
        sbuf("ysel", [128, 8 * CSEL], BF16)
        sbuf("yselT", [128, 2 * D], BF16)
        sbuf("arena", [128, 8192], BF16)
        sbuf("out_ffn", [128, 4 * D], F32)
        sbuf("up_buf0", [128, 2048], BF16)
        sbuf("up_buf1", [128, 2048], BF16)
        sbuf("up_buf2", [128, 2048], BF16)
        sbuf("up_buf3", [128, 2048], BF16)
        sbuf("dn_buf0", [128, 2048], BF16)
        sbuf("dn_buf1", [128, 2048], BF16)
        sbuf("dn_buf2", [128, 2048], BF16)
        sbuf("dn_buf3", [128, 2048], BF16)
        sbuf("tmp0", [128, 128], F32)
        sbuf("attn", [128, 8 * GRP], BF16)
        sbuf("y_g", [128, 8 * GRP], BF16)
        sbuf("q0", [128, GRP], BF16)
        sbuf("pT0", [128, 1024], BF16)
        sbuf("pT1", [128, 1024], BF16)
        sbuf("rz", [128, 512], BF16)
        sbuf("zb", [128, GRP], BF16)
        sbuf("kvb", [128, GRP], BF16)
        sbuf("x0", [128, D], F32)
        sbuf("sq0", [128, D], BF16)
        sbuf("lnstat", [128, 16], F32)
        sbuf("epsv", [128, 1], F32)
        sbuf("conv0", [128, 4096], BF16)
        sbuf("conv1", [128, 4096], BF16)
        sbuf("ev", [128, 4096], BF16)

        PS = self.ps
        for k in ("F0", "F1", "F2", "F3", "F4", "F5"):
            PS[k] = es.enter_context(nc.psum_tensor(k, [128, 512], F32))
        PS["TB"] = es.enter_context(nc.psum_tensor("TB", [128, 2048], BF16))
        self.sems = {n: es.enter_context(nc.semaphore("s_" + n)) for n in self.ch}

    def mm(self, out_ap, lhsT, rhs, start, stop, waits=(), tile_position=None):
        ch = self.ch["pe"]

        def fn(eng, o=out_ap, l=lhsT, r=rhs, a=start, b=stop, tp=tile_position):
            return eng.matmul(o, l, r, start=a, stop=b, tile_position=tp)
        self.P.op("tensor", fn, waits=waits, incs=((ch, 1),))
        return ch.n

    def tr(self, out_ap, in_ap, ident, waits=()):
        ch = self.ch["pe"]

        def fn(eng, o=out_ap, i=in_ap, d=ident):
            return eng.transpose(o, i, d)
        self.P.op("tensor", fn, waits=waits, incs=((ch, 1),))
        return ch.n

    def act(self, out_ap, in_ap, func, waits=(), bias=None, scale=None):
        ch = self.ch["act"]

        def fn(eng, o=out_ap, i=in_ap, f=func, b=bias, s=scale):
            kw = {}
            if b is not None:
                kw["bias"] = b
            if s is not None:
                kw["scale"] = s
            return eng.activation(o, i, f, **kw)
        self.P.op("scalar", fn, waits=waits, incs=((ch, 1),))
        return ch.n

    def acopy(self, out_ap, in_ap, waits=()):
        ch = self.ch["act"]

        def fn(eng, o=out_ap, i=in_ap):
            return eng.copy(o, i)
        self.P.op("scalar", fn, waits=waits, incs=((ch, 1),))
        return ch.n

    def dve(self, kind, waits=(), allow_lp=False, **kw):
        ch = self.ch["dve"]
        nc = self.nc

        def fn(eng, kind=kind, kw=kw, allow_lp=allow_lp):
            if allow_lp:
                with nc.allow_low_precision(reason="bf16 1/Z: feeds bf16 matmul"):
                    return getattr(eng, kind)(**kw)
            return getattr(eng, kind)(**kw)
        self.P.op("vector", fn, waits=waits, incs=((ch, 1),))
        return ch.n

    def pool(self, kind, waits=(), **kw):
        ch = self.ch["pool"]

        def fn(eng, kind=kind, kw=kw):
            return getattr(eng, kind)(**kw)
        self.P.op("gpsimd", fn, waits=waits, incs=((ch, 1),))
        return ch.n

    def dma(self, queue, chname, out_ap, in_ap, waits=()):
        ch = self.ch[chname]

        def fn(eng, o=out_ap, i=in_ap):
            return eng.dma_start(out=o, in_=i)
        self.P.op(queue, fn, waits=waits, incs=((ch, 16),))
        return ch.n

    def barrier(self, eng, waits):
        self.P.op(eng, None, waits=waits)

    def tbank(self):
        slot = self.tring_i % 2
        self.tring_i += 1
        bank = self.ps["TB"][:, slot * 1024:(slot + 1) * 1024]
        w = self.tring_free[slot]
        self.tring_free[slot] = None
        return bank, slot, ([w] if w is not None else [])

    def tbank_release(self, slot, wait):
        self.tring_free[slot] = wait

    def tbank_pair(self):
        if self.tring_i % 2 == 1:
            _, s, w = self.tbank()
            self.tbank_release(s, w[0] if w else None)
        _, s0, w0 = self.tbank()
        _, s1, w1 = self.tbank()
        assert s0 == 0 and s1 == 1
        return w0 + w1

    def fb(self, name):
        w = self.bank_free[name]
        self.bank_free[name] = None
        return self.ps[name], ([w] if w is not None else [])

    def fb_release(self, name, wait):
        self.bank_free[name] = wait

    def emit_prologue(self):
        S, D_, ch = self.sb, self.dram, self.ch
        loads = [
            (S["eyeb"][:, :], D_["eyeb"].ap()),
            (S["eyef"][:, :], D_["eyef"].ap()),
            (S["maskc"][:, :], D_["maskc"].ap()),
            (S["dup64"][:, :], D_["dup64"].ap()),
            (S["jrow"][:, :], D_["jrow"].ap()),
            (S["ysel"][0:N_EXP, 0:D][:, :], D_["gw"].ap()),
            (S["ebrow"][:, :], D_["eb"].ap().rearrange("(o e) -> o e", o=1)),
            (S["wcd"][:, :].rearrange("p (c l) -> p c l", l=LAT),
             D_["wcd"].ap().rearrange("(c p) l -> p c l", p=128)),
            (S["wq"][:, :].rearrange("p (c q) -> p c q", q=D),
             D_["wq"].ap().rearrange("(c p) q -> p c q", p=128)),
            (S["wkv"][:, :].rearrange("p (c k) -> p c k", k=2 * HD),
             D_["wkv"].ap().rearrange("(c p) k -> p c k", p=128)),
            (S["wproj"][:, :].rearrange("p (c o) -> p c o", o=D),
             D_["wproj"].ap().rearrange("(c p) o -> p c o", p=128)),
        ]
        for o, i in loads:
            self.dma("gpsimd", "mld", o, i)
        self.mld_all = ch["mld"].n
        for name, v in (("ones1f", 1.0), ("onesb", 1.0), ("onescol", 1.0),
                        ("ones8", 1.0), ("carry", 0.0),
                        ("epsv", 1e-5)):
            self.pool("memset", ap=S[name][:, :], constant=float(v))
        self.pool("memset", ap=S["v_aug"][:, :], constant=1.0)
        self.pool_init = ch["pool"].n
        mld_w = (ch["mld"], self.mld_all)
        pool_w = (ch["pool"], self.pool_init)

        bank, slot, bw = self.tbank()
        last = None
        for c in range(8):
            last = self.tr(bank[:, c * 8:(c + 1) * 8],
                           S["ysel"][0:N_EXP, 0:D][0:8, c * 128:(c + 1) * 128],
                           S["eyeb"][0:8, 0:8],
                           waits=([mld_w] + bw) if c == 0 else ())
        ev = self.dve("tensor_copy", waits=[(ch["pe"], last)],
                      out=S["gate_wT"][:, :], in_=bank[:, 0:64])
        self.tbank_release(slot, (ch["dve"], ev))
        self.gate_wT_ready = (ch["dve"], ev)

        bank, bw = self.fb("F0")
        m = self.mm(bank[0:128, 0:N_EXP], S["ones1f"][:, :], S["ebrow"][:, :],
                    True, True, waits=[mld_w, pool_w] + bw)
        a = self.acopy(S["bias_b"][:, :], bank[0:128, 0:N_EXP],
                       waits=[(ch["pe"], m)])
        self.fb_release("F0", (ch["act"], a))
        self.bias_b_ready = (ch["act"], a)
        self.conv_free = [None, None]
        self.ev_free = None
        self.conv_mark = {}

    def emit_conv_expert(self, e):
        S, D_, ch = self.sb, self.dram, self.ch
        chp, chd, chc, chs = ch["pe"], ch["dve"], ch["cvl"], ch["cst"]
        mld_w = (ch["mld"], self.mld_all)
        convs = [S["conv0"], S["conv1"]]
        for q in range(8):
            r = q % 2
            ld = self.dma("gpsimd", "cvl",
                          convs[r][:, :].rearrange("p (a h) -> p a h", h=1024),
                          D_["wup"].ap()[e, q * 512:(q + 1) * 512, :].rearrange(
                              "(a p) h -> p a h", p=128),
                          waits=[self.conv_free[r], mld_w])
            for a in range(4):
                s = q * 4 + a
                g, sl = s // 4, s % 4
                hf, s2 = sl // 2, s % 2
                bank, slot, bw = self.tbank()
                last = None
                for c in range(8):
                    last = self.tr(bank[:, c * 128:(c + 1) * 128],
                                   convs[r][:, a * 1024 + c * 128:a * 1024 + (c + 1) * 128],
                                   S["eyeb"][:, :],
                                   waits=([(chc, ld)] + bw) if c == 0 else ())
                evw = [(chp, last)]
                if sl == 0 and self.ev_free is not None:
                    evw.append(self.ev_free)
                out_ap = S["ev"][:, hf * 2048:(hf + 1) * 2048].rearrange(
                    "p (c i) -> p c i", i=256)[:, :, s2 * 128:(s2 + 1) * 128]
                in_ap = bank[:, :].rearrange("p (c i) -> p c i", i=128)
                evv = self.dve("tensor_copy", waits=evw, out=out_ap, in_=in_ap)
                self.tbank_release(slot, (chd, evv))
                if sl == 3:
                    st = self.dma("sync", "cst", D_["up_s"].ap()[e, g], S["ev"][:, :],
                                  waits=[(chd, evv)])
                    self.ev_free = (chs, st)
            self.conv_free[r] = (chp, last)
        self.conv_mark[(e, "up")] = (chs, chs.n)
        for s in range(8):
            r = s % 2
            ld = self.dma("gpsimd", "cvl", convs[r][:, :],
                          D_["wdn"].ap()[e, s * 128:(s + 1) * 128, :],
                          waits=[self.conv_free[r]])
            last_ev = None
            last_tr = None
            for g8 in range(4):
                bank, slot, bw = self.tbank()
                last_tr = None
                for k in range(8):
                    ic = g8 * 8 + k
                    last_tr = self.tr(bank[:, k * 128:(k + 1) * 128],
                                      convs[r][:, ic * 128:(ic + 1) * 128],
                                      S["eyeb"][:, :],
                                      waits=([(chc, ld)] + bw) if k == 0 else ())
                evw = [(chp, last_tr)]
                if g8 == 0 and self.ev_free is not None:
                    evw.append(self.ev_free)
                last_ev = self.dve("tensor_copy", waits=evw,
                                   out=S["ev"][:, g8 * 1024:(g8 + 1) * 1024],
                                   in_=bank[:, :])
                self.tbank_release(slot, (chd, last_ev))
            self.conv_free[r] = (chp, last_tr)
            st = self.dma("sync", "cst", D_["dn_s"].ap()[e, s], S["ev"][:, :],
                          waits=[(chd, last_ev)])
            self.ev_free = (chs, st)
        self.conv_mark[(e, "dn")] = (chs, chs.n)

    def emit_ln(self, g):
        S, D_, ch = self.sb, self.dram, self.ch
        chx, chd, cha, chp = ch["x"], ch["dve"], ch["act"], ch["pe"]
        ev = None
        for tt in range(4):
            r = 0
            xt = S["x0"]
            sq = S["sq0"]
            xtb = tt * D
            ld = self.dma("scalar", "x", xt[:, :],
                          D_["x"].ap()[g * GRP + tt * 128:g * GRP + (tt + 1) * 128, :],
                          waits=[self.marks.get(("x_free", r))])
            ldw = (chx, ld)
            cc = [r * 8 + k for k in range(8)]
            st = S["lnstat"]
            d1 = self.dve("reduce_sum", waits=[ldw], out=st[:, cc[0]:cc[0] + 1],
                          in_=xt[:, :], axis=AX.X)
            a1 = self.act(sq[:, :], xt[:, :], AF.Square,
                          waits=[ldw, self.marks.get(("sq_free", r))])
            d2 = self.dve("reduce_sum", waits=[(cha, a1)], out=st[:, cc[1]:cc[1] + 1],
                          in_=sq[:, :], axis=AX.X)
            self.marks[("sq_free", r)] = (chd, d2)
            d3 = self.dve("tensor_scalar", waits=[(chd, d1)],
                          out=st[:, cc[2]:cc[2] + 1], in0=st[:, cc[0]:cc[0] + 1],
                          scalar1=1.0 / D, scalar2=None, op0=ALU.mult)
            d4 = self.dve("tensor_scalar", waits=[(chd, d2)],
                          out=st[:, cc[3]:cc[3] + 1], in0=st[:, cc[1]:cc[1] + 1],
                          scalar1=1.0 / D, scalar2=None, op0=ALU.mult)
            d5 = self.dve("tensor_tensor", waits=[(chd, d3)],
                          out=st[:, cc[4]:cc[4] + 1], in0=st[:, cc[2]:cc[2] + 1],
                          in1=st[:, cc[2]:cc[2] + 1], op=ALU.mult)
            d6 = self.dve("tensor_tensor", waits=[(chd, d5)],
                          out=st[:, cc[5]:cc[5] + 1], in0=st[:, cc[3]:cc[3] + 1],
                          in1=st[:, cc[4]:cc[4] + 1], op=ALU.subtract)
            a2 = self.act(st[:, cc[6]:cc[6] + 1], st[:, cc[5]:cc[5] + 1], AF.Ln,
                          waits=[(chd, d6)], bias=S["epsv"][:, 0:1])
            a3 = self.act(st[:, cc[7]:cc[7] + 1], st[:, cc[6]:cc[6] + 1], AF.Exp,
                          waits=[(cha, a2)], scale=-0.5)
            d7 = self.dve("tensor_scalar",
                          waits=[(cha, a3), self.marks.get("xn_tok_free")],
                          out=S["xn_tok"][:, xtb:xtb + D],
                          in0=xt[:, :], scalar1=st[:, cc[2]:cc[2] + 1],
                          scalar2=st[:, cc[7]:cc[7] + 1], op0=ALU.subtract,
                          op1=ALU.mult)
            self.marks[("x_free", r)] = (chd, d7)
            bank, slot, bw = self.tbank()
            last = None
            for c in range(8):
                last = self.tr(bank[:, c * 128:(c + 1) * 128],
                               S["xn_tok"][:, xtb + c * 128:xtb + (c + 1) * 128],
                               S["eyeb"][:, :],
                               waits=([(chd, d7)] + bw) if c == 0 else ())
            ev = self.dve(
                "tensor_copy", waits=[(chp, last)],
                out=S["xn"][:, :].rearrange("p (c t) -> p c t", t=GRP)[:, :, tt * 128:(tt + 1) * 128],
                in_=bank[:, :].rearrange("p (c t) -> p c t", t=128))
            self.tbank_release(slot, (chd, ev))
        self.xn_ready = (chd, ev)

    def emit_latent(self, g):
        S, ch = self.sb, self.ch
        chp, cha = ch["pe"], ch["act"]
        names = ["F0", "F1"]
        a = None
        for l in range(4):
            bank, bw = self.fb(names[l % 2])
            m = None
            for c in range(8):
                m = self.mm(bank[:, :],
                            S["wcd"][:, c * 512 + l * 128:c * 512 + (l + 1) * 128],
                            S["xn"][:, c * GRP:(c + 1) * GRP], c == 0, c == 7,
                            waits=([self.xn_ready] + bw) if c == 0 else ())
            a = self.acopy(S["latent"][:, l * GRP:(l + 1) * GRP], bank[:, :],
                           waits=[(chp, m)])
            self.fb_release(names[l % 2], (cha, a))
        self.latent_ready = (cha, a)

    def emit_kv(self, g):
        S, ch = self.sb, self.ch
        chp, cha, chd = ch["pe"], ch["act"], ch["dve"]
        t0 = (g % 4) * GRP
        ow = self.marks.get(("attn_done", 3)) if g == 4 else None
        bank, bw = self.fb("F2")
        m = None
        for l in range(4):
            m = self.mm(bank[:, :], S["wkv"][:, l * 128:(l + 1) * 128],
                        S["latent"][:, l * GRP:(l + 1) * GRP], l == 0, l == 3,
                        waits=([self.latent_ready] + bw) if l == 0 else ())
        a1 = self.acopy(S["kvb"][:, :], bank[:, :],
                        waits=[(chp, m), self.marks.get("kvb_free")])
        self.fb_release("F2", (cha, a1))
        bank3, bw3 = self.fb("F3")
        m2 = self.mm(bank3[:, :], S["dup64"][:, :], S["kvb"][0:64, :], True, True,
                     waits=[(cha, a1)] + bw3)
        a2 = self.acopy(S["k_dup"][:, t0:t0 + GRP], bank3[:, :],
                        waits=[(chp, m2), ow])
        self.fb_release("F3", (cha, a2))
        bank, slot, tbw = self.tbank()
        last = None
        for jj in range(4):
            last = self.tr(bank[:, jj * 64:(jj + 1) * 64],
                           S["kvb"][64:128, jj * 128:(jj + 1) * 128],
                           S["eyeb"][64:128, 64:128],
                           waits=([(cha, a1)] + tbw) if jj == 0 else ())
        j0 = t0 // 128
        vw = [(chp, last), (self.ch["pool"], self.pool_init), ow]
        evv = self.dve("tensor_copy", waits=vw,
                       out=S["v_aug"][:, :].rearrange("p (j c) -> p j c", c=65)[:, j0:j0 + 4, 0:64],
                       in_=bank[:, 0:256].rearrange("p (j c) -> p j c", c=64))
        self.tbank_release(slot, (chd, evv))
        self.marks[("kv_ready", g % 4)] = (cha, a2)
        self.marks[("v_ready", g % 4)] = (chd, evv)
        self.marks["kvb_free"] = (chp, last)

    def emit_gate(self, g):
        S, ch = self.sb, self.ch
        chp, cha, chd = ch["pe"], ch["act"], ch["dve"]
        E = N_EXP
        W = S["tmp0"]
        for tt in range(4):
            b = tt * 40
            R = S["routg"]
            mask1 = R[:, b:b + 8]
            mask2 = R[:, b + 8:b + 16]
            m12 = R[:, b + 16:b + 24]
            wsel = R[:, b + 24:b + 32]
            w1c = R[:, b + 32:b + 33]
            w2c = R[:, b + 33:b + 34]
            bank, bw = self.fb("F3")
            m = None
            for c in range(8):
                m = self.mm(bank[0:128, 0:E],
                            S["xn"][:, c * GRP + tt * 128:c * GRP + (tt + 1) * 128],
                            S["gate_wT"][:, c * E:(c + 1) * E], c == 0, c == 7,
                            waits=([self.xn_ready, self.gate_wT_ready,
                                    self.bias_b_ready] + bw) if c == 0 else ())
            wg = W[:, 0:E]
            e_t = W[:, E:2 * E]
            w_ = W[:, 2 * E:3 * E]
            m1 = W[:, 3 * E:3 * E + 1]
            nm = W[:, 3 * E + 1:3 * E + 2]
            w2 = W[:, 5 * E:6 * E]
            m2 = W[:, 6 * E:6 * E + 1]
            dsum = W[:, 9 * E:9 * E + 1]
            rd = W[:, 9 * E + 1:9 * E + 2]
            Z = W[:, 9 * E + 2:9 * E + 3]
            rZ = W[:, 9 * E + 3:9 * E + 4]
            t8 = W[:, 11 * E:12 * E]
            d0 = self.dve("tensor_tensor",
                          waits=[(chp, m), self.marks.get(("tmp_free", 0)),
                                 self.marks.get(("routg_free", tt))],
                          out=wg, in0=bank[0:128, 0:E], in1=S["bias_b"][:, :],
                          op=ALU.add)
            self.fb_release("F3", (chd, d0))
            dm = self.dve("reduce_max", waits=[(chd, d0)], out=nm, in_=wg,
                          axis=AX.X, negate=True)
            a1 = self.act(e_t, wg, AF.Exp, waits=[(chd, dm)], bias=nm)
            d2 = self.dve("reduce_sum", waits=[(cha, a1)], out=Z, in_=e_t, axis=AX.X)
            d3 = self.dve("reciprocal", waits=[(chd, d2)], out=rZ, in_=Z)
            d4 = self.dve("tensor_scalar", waits=[(chd, d3)], out=w_, in0=e_t,
                          scalar1=rZ, scalar2=None, op0=ALU.mult)
            d5 = self.dve("reduce_max", waits=[(chd, d4)], out=m1, in_=w_, axis=AX.X)
            d6 = self.dve("tensor_scalar", waits=[(chd, d5)], out=mask1, in0=w_,
                          scalar1=m1, scalar2=None, op0=ALU.is_ge)
            d7 = self.dve("scalar_tensor_tensor", waits=[(chd, d6)], out=w2,
                          in0=mask1, scalar=-1e30, in1=w_, op0=ALU.mult, op1=ALU.add)
            d8 = self.dve("reduce_max", waits=[(chd, d7)], out=m2, in_=w2, axis=AX.X)
            d9 = self.dve("tensor_scalar", waits=[(chd, d8)], out=mask2, in0=w2,
                          scalar1=m2, scalar2=None, op0=ALU.is_ge)
            d10 = self.dve("tensor_tensor", waits=[(chd, d9)], out=m12, in0=mask1,
                           in1=mask2, op=ALU.add)
            d11 = self.dve("tensor_tensor", waits=[(chd, d8)], out=dsum, in0=m1,
                           in1=m2, op=ALU.add)
            d12 = self.dve("tensor_scalar", waits=[(chd, d11)], out=dsum, in0=dsum,
                           scalar1=1e-6, scalar2=None, op0=ALU.add)
            d13 = self.dve("reciprocal", waits=[(chd, d12)], out=rd, in_=dsum)
            d14 = self.dve("tensor_tensor", waits=[(chd, d13)], out=w1c, in0=m1,
                           in1=rd, op=ALU.mult)
            d15 = self.dve("tensor_tensor", waits=[(chd, d14)], out=w2c, in0=m2,
                           in1=rd, op=ALU.mult)
            d16 = self.dve("tensor_scalar", waits=[(chd, d15)], out=t8, in0=mask1,
                           scalar1=w1c, scalar2=None, op0=ALU.mult)
            d17 = self.dve("tensor_scalar", waits=[(chd, d16)], out=wsel, in0=mask2,
                           scalar1=w2c, scalar2=None, op0=ALU.mult)
            d18 = self.dve("tensor_tensor", waits=[(chd, d17)], out=wsel, in0=wsel,
                           in1=t8, op=ALU.add)
            self.marks[("tmp_free", 0)] = (chd, d18)
            self.marks[("gatechain", tt)] = (chd, d18)
        bank, slot, bw = self.tbank()
        b32 = bank[:, :].bitcast(F32)
        last = None
        for tt in range(4):
            last = self.tr(b32[0:8, tt * 128:(tt + 1) * 128],
                           S["routg"][:, tt * 40 + 16:tt * 40 + 24], S["eyef"][:, :],
                           waits=([self.marks[("gatechain", 3)],
                                   self.marks.get("mrow_free")] + bw) if tt == 0 else ())
        am = self.acopy(S["mrow"][:, :], b32[0:8, 0:512], waits=[(chp, last)])
        self.tbank_release(slot, (cha, am))
        s1 = self.dve("tensor_tensor_scan",
                      waits=[(cha, am), self.marks.get("carry_free")],
                      out=S["posrow"][:, :], data0=S["ones8"][:, :],
                      data1=S["mrow"][:, :],
                      initial=0.0, op0=ALU.mult, op1=ALU.add)
        s2 = self.dve("tensor_copy", waits=[(chd, s1)], out=S["carry"][:, :],
                      in_=S["posrow"][:, 511:512])
        self.marks["carry_free"] = (chd, s2)
        s3 = self.dve("tensor_tensor", waits=[(chd, s2)], out=S["mrow"][:, :],
                      in0=S["posrow"][:, :], in1=S["mrow"][:, :], op=ALU.subtract)
        mr_last = None
        for tt in range(4):
            bank, slot, bw = self.tbank()
            b32 = bank[:, :].bitcast(F32)
            trp = self.tr(b32[:, 0:8], S["mrow"][0:8, tt * 128:(tt + 1) * 128],
                          S["eyef"][0:8, 0:8],
                          waits=[(chd, s3), self.marks.get("ptok_free")] + bw)
            ap = self.acopy(S["ptok"][:, tt * 8:(tt + 1) * 8], b32[:, 0:8],
                            waits=[(chp, trp)])
            self.tbank_release(slot, (cha, ap))
            mr_last = trp
        self.marks["mrow_free"] = (chp, mr_last)
        self.ptok_ready = (cha, ap)

    def tbank32(self, bank):
        return bank[:, :].bitcast(F32)[:, 0:512]

    def build_attention_units(self, g):
        S, ch = self.sb, self.ch
        chp, cha, chd = ch["pe"], ch["act"], ch["dve"]
        t0 = (g % 4) * GRP
        J = t0 // 128 + 4
        diag0 = t0 // 128
        kvw = [self.marks[("kv_ready", q)] for q in range(g % 4 + 1)]
        vw = [self.marks[("v_ready", q)] for q in range(g % 4 + 1)]
        units = []
        st = {"y_marks": []}

        def q_unit(p):
            def fn():
                qb = S["q0"]
                bank, slot, bw = self.tbank()
                b32 = self.tbank32(bank)
                m = None
                for l in range(4):
                    m = self.mm(b32, S["wq"][:, l * D + p * 128:l * D + (p + 1) * 128],
                                S["latent"][:, l * GRP:(l + 1) * GRP], l == 0, l == 3,
                                waits=([self.latent_ready,
                                        self.marks.get(("q_free", 0))] + bw)
                                if l == 0 else ())
                aq = self.acopy(qb[:, :], b32, waits=[(chp, m)])
                self.tbank_release(slot, (cha, aq))
                st[("qw", p)] = (cha, aq)
            return fn

        def score_unit(p, j):
            def fn():
                qb = S["q0"]
                pT = S["pT0"] if j % 2 == 0 else S["pT1"]
                bw = self.tbank_pair()
                TBf = self.ps["TB"][:, :].bitcast(F32)
                bA = TBf[:, 0:512]
                bB = TBf[:, 512:1024]
                mA = self.mm(bA, S["k_dup"][0:64, j * 128:(j + 1) * 128],
                             qb[0:64, :], True, True,
                             waits=[st[("qw", p)]] + kvw + bw)
                mB = self.mm(bB, S["k_dup"][64:128, j * 128:(j + 1) * 128],
                             qb[64:128, :], True, True)
                expw = [(chp, mB)]
                if j >= diag0:
                    o = (j - diag0) * 128
                    msl = S["maskc"][:, 384 - o:896 - o]
                    dA = self.dve("tensor_tensor", waits=[(chp, mA)], out=bA,
                                  in0=bA, in1=msl, op=ALU.add)
                    dB = self.dve("tensor_tensor", waits=[(chp, mB)], out=bB,
                                  in0=bB, in1=msl, op=ALU.add)
                    expw = [(chd, dB)]
                pfree = self.marks.get(("pT_free", j % 2))
                eA = self.act(pT[:, 0:1024], TBf[:, 0:1024], AF.Exp,
                              waits=expw + [pfree], scale=0.125)
                self.tbank_release(0, (cha, eA))
                self.tbank_release(1, (cha, eA))
                st[("exp", p, j)] = [(cha, eA)]
            return fn

        def av_unit(p, j):
            def fn():
                jr = j % 2
                pT = S["pT0"] if jr == 0 else S["pT1"]
                avw = list(st.pop(("exp", p, j)))
                if j == 0:
                    _, bw2 = self.fb("F4")
                    _, bw3 = self.fb("F5")
                    avw = avw + vw + bw2 + bw3
                m1_ = self.mm(self.ps["F4"][0:65, :],
                              S["v_aug"][:, j * 65:j * 65 + 65], pT[:, 0:512],
                              j == 0, j == J - 1, waits=avw)
                m2_ = self.mm(self.ps["F5"][64:128, :],
                              S["v_aug"][:, j * 65:j * 65 + 64], pT[:, 512:1024],
                              j == 0, j == J - 1, tile_position=(0, 64))
                m3 = self.mm(self.ps["F5"][0:1, :], S["onescol"][:, :],
                             pT[:, 512:1024], j == 0, j == J - 1)
                self.marks[("pT_free", jr)] = (chp, m3)
                st[("av", p)] = (chp, m3)
            return fn

        def norm_a_unit(p):
            def fn():
                m3 = st[("av", p)]
                self.marks[("q_free", 0)] = m3
                d1 = self.dve("reciprocal", waits=[m3, self.marks.get("rz_free")],
                              allow_lp=True,
                              out=S["rz"][64:65, 0:512], in_=self.ps["F4"][64:65, :])
                d2 = self.dve("reciprocal", waits=[m3], allow_lp=True,
                              out=S["rz"][0:1, 0:512], in_=self.ps["F5"][0:1, :])
                st[("rec", p)] = (d1, d2)
            return fn

        def norm_b_unit(p):
            def fn():
                d1, d2 = st.pop(("rec", p))
                bkA, slA, bwA = self.tbank()
                zA = self.tbank32(bkA)
                bkB, slB, bwB = self.tbank()
                zB = self.tbank32(bkB)
                mzA = self.mm(zA[0:64, :], S["onesb"][64:65, 0:64],
                              S["rz"][64:65, 0:512], True, True,
                              waits=[(chd, d1)] + bwA)
                mzB = self.mm(zB[64:128, :], S["onesb"][0:1, 0:64],
                              S["rz"][0:1, 0:512], True, True,
                              waits=[(chd, d2)] + bwB, tile_position=(0, 64))
                self.marks["rz_free"] = (chp, mzB)
                azA = self.acopy(S["zb"][0:64, :], zA[0:64, :],
                                 waits=[(chp, mzA), self.marks.get("zb_free")])
                azB = self.acopy(S["zb"][64:128, :], zB[64:128, :],
                                 waits=[(chp, mzB)])
                self.tbank_release(slA, (cha, azA))
                self.tbank_release(slB, (cha, azB))
                dyA = self.dve("tensor_tensor", waits=[(cha, azA)],
                               out=S["y_g"][0:64, p * GRP:(p + 1) * GRP],
                               in0=self.ps["F4"][0:64, :], in1=S["zb"][0:64, :],
                               op=ALU.mult)
                dyB = self.dve("tensor_tensor", waits=[(cha, azB)],
                               out=S["y_g"][64:128, p * GRP:(p + 1) * GRP],
                               in0=self.ps["F5"][64:128, :], in1=S["zb"][64:128, :],
                               op=ALU.mult)
                self.marks["zb_free"] = (chd, dyB)
                self.fb_release("F4", (chd, dyA))
                self.fb_release("F5", (chd, dyB))
                st["y_marks"] += [(chd, dyA), (chd, dyB)]
                if g % 4 == 3 and p == N_PAIR - 1:
                    self.marks[("attn_done", 3)] = st[("av", p)]
            return fn

        for p in range(N_PAIR):
            units.append(q_unit(p))
            for j in range(J):
                units.append(score_unit(p, j))
                units.append(av_unit(p, j))
            units.append(norm_a_unit(p))
            units.append(norm_b_unit(p))
        self._attn_units = units
        self._attn_state = st
        self._slots_left = N_EXP * 28
        return units

    def pop_unit(self):
        n = self._slots_left
        self._slots_left = max(1, n - 1)
        import math
        k = max(1, math.ceil(len(self._attn_units) / max(1, n)))
        for _ in range(min(k, len(self._attn_units))):
            self._attn_units.pop(0)()

    def emit_proj(self, g):
        S, ch = self.sb, self.ch
        chp, cha = ch["pe"], ch["act"]
        y_marks = self._attn_state["y_marks"]
        names = ["F4", "F5"]
        attn_marks = []
        for ho in range(8):
            bank, bw = self.fb(names[ho % 2])
            m = None
            for c in range(8):
                m = self.mm(bank[:, :],
                            S["wproj"][:, c * D + ho * 128:c * D + (ho + 1) * 128],
                            S["y_g"][:, c * GRP:(c + 1) * GRP], c == 0, c == 7,
                            waits=(y_marks + bw) if c == 0 else ())
            a = self.acopy(S["attn"][:, ho * GRP:(ho + 1) * GRP], bank[:, :],
                           waits=[(chp, m)])
            self.fb_release(names[ho % 2], (cha, a))
            attn_marks.append((cha, a))
        self.attn_ready = attn_marks

    def emit_pbuild(self, g, e):
        S, ch = self.sb, self.ch
        chd = ch["dve"]
        W = S["tmp0"]
        p_marks = []
        pw_marks = []
        for tt in range(4):
            b = tt * 40
            qcol = W[:, 14 * N_EXP:14 * N_EXP + 1]
            q1 = self.dve("tensor_scalar",
                          waits=[self.ptok_ready, self.marks.get(("tmp_free", 0)),
                                 self.marks.get(("Pg_free", tt))],
                          out=qcol, in0=S["ptok"][:, tt * 8 + e:tt * 8 + e + 1],
                          scalar1=4096.0, scalar2=None, op0=ALU.add)
            q2 = self.dve("scalar_tensor_tensor", waits=[(chd, q1)], out=qcol,
                          in0=S["routg"][:, b + 16 + e:b + 17 + e], scalar=-4096.0,
                          in1=qcol, op0=ALU.mult, op1=ALU.add)
            q3 = self.dve("tensor_scalar", waits=[(chd, q2)],
                          out=S["Pg"][:, tt * CSEL:(tt + 1) * CSEL],
                          in0=S["jrow"][:, :], scalar1=qcol, scalar2=None,
                          op0=ALU.is_equal)
            q4 = self.dve("tensor_scalar", waits=[(chd, q3)],
                          out=S["Pw"][:, tt * CSEL:(tt + 1) * CSEL],
                          in0=S["Pg"][:, tt * CSEL:(tt + 1) * CSEL],
                          scalar1=S["routg"][:, b + 24 + e:b + 25 + e],
                          scalar2=None, op0=ALU.mult)
            self.marks[("tmp_free", 0)] = (chd, q2)
            p_marks.append((chd, q3))
            pw_marks.append((chd, q4))
        self._pm[(g, e)] = (p_marks, pw_marks)

    def emit_ffn(self, g, e):
        S, D_, ch = self.sb, self.dram, self.ch
        chp, cha, chd = ch["pe"], ch["act"], ch["dve"]
        chu, chn = ch["wup"], ch["wdn"]
        h = S["arena"]
        conv_u = self.conv_mark[(e, "up")]
        conv_d = self.conv_mark[(e, "dn")]
        up_bufs = [S["up_buf0"], S["up_buf1"], S["up_buf2"], S["up_buf3"]]
        dn_bufs = [S["dn_buf0"], S["dn_buf1"], S["dn_buf2"], S["dn_buf3"]]

        def load_up(k):
            gg, hf = k // 2, k % 2
            ld = self.dma("gpsimd", "wup", up_bufs[k % 4][:, :],
                          D_["up_s"].ap()[e, gg][:, hf * 2048:(hf + 1) * 2048],
                          waits=[self.marks.get(("upb_free", k % 4)), conv_u])
            self.marks[("upb_loaded", k)] = (chu, ld)

        def load_dn(k):
            t, hf = k // 2, k % 2
            ld = self.dma("sync", "wdn", dn_bufs[k % 4][:, :],
                          D_["dn_s"].ap()[e, t][:, hf * 2048:(hf + 1) * 2048],
                          waits=[self.marks.get(("dnb_free", k % 4)), conv_d])
            self.marks[("dnb_loaded", k)] = (chn, ld)

        for k in range(4):
            load_up(k)
        if (g, e) not in self._pm:
            self.emit_pbuild(g, e)
        p_marks, pw_marks = self._pm.pop((g, e))
        pbanks = ["F0", "F1"]
        gxn_marks = []
        for dch in range(8):
            bank, bw = self.fb(pbanks[dch % 2])
            m = None
            for tt in range(4):
                w = ()
                if tt == 0:
                    w = p_marks + bw + [self.xn_ready]
                m = self.mm(bank[:, 0:CSEL],
                            S["xn_tok"][:, tt * D + dch * 128:tt * D + (dch + 1) * 128],
                            S["Pg"][:, tt * CSEL:(tt + 1) * CSEL],
                            tt == 0, tt == 3, waits=w)
            a = self.dve("tensor_copy", waits=[(chp, m)],
                         out=S["gxn"][:, dch * CSEL:(dch + 1) * CSEL],
                         in_=bank[:, 0:CSEL])
            self.fb_release(pbanks[dch % 2], (chd, a))
            gxn_marks.append((chd, a))
        self.marks["xn_tok_used"] = (chp, m)
        self.pop_unit()
        ptw_marks = []
        for jh in range(2):
            bank, slot, bw = self.tbank()
            last = None
            for tt in range(4):
                last = self.tr(bank[:, tt * 128:(tt + 1) * 128],
                               S["Pw"][:, tt * CSEL + jh * 128:tt * CSEL + (jh + 1) * 128],
                               S["eyeb"][:, :],
                               waits=(pw_marks + bw +
                                      [self.marks.get("PTw_free")]) if tt == 0 else ())
            ev = self.dve("tensor_copy", waits=[(chp, last)],
                          out=S["PTw"][:, jh * GRP:(jh + 1) * GRP],
                          in_=bank[:, 0:512])
            self.tbank_release(slot, (chd, ev))
            ptw_marks.append((chd, ev))
            self.marks[("Pg_free", 10 + jh)] = (chp, last)
        for tt in range(4):
            self.marks[("Pg_free", tt)] = (chp, last)
        if e + 1 < N_EXP:
            self.emit_pbuild(g, e + 1)
        self.pop_unit()
        ubanks = ["F0", "F1"]
        gelu_last = None
        for k in range(16):
            for it2 in range(2):
                itile = k * 2 + it2
                bname = ubanks[itile % 2]
                bank, bw = self.fb(bname)
                m = None
                for c in range(8):
                    w = ()
                    if c == 0:
                        w = [self.marks[("upb_loaded", k)]] + gxn_marks + bw
                    m = self.mm(bank[:, 0:CSEL],
                                up_bufs[k % 4][:, c * 256 + it2 * 128:
                                               c * 256 + it2 * 128 + 128],
                                S["gxn"][:, c * CSEL:(c + 1) * CSEL],
                                c == 0, c == 7, waits=w)
                a = self.act(h[:, itile * CSEL:(itile + 1) * CSEL], bank[:, 0:CSEL],
                             AF.Gelu, waits=[(chp, m)])
                self.fb_release(bname, (cha, a))
                gelu_last = a
            self.marks[("upb_free", k % 4)] = (chp, m)
            if k + 4 < 16:
                load_up(k + 4)
            self.pop_unit()
        for k in range(4):
            load_dn(k)
        dbanks = ["F2", "F3"]
        ysel_marks = []
        for t in range(8):
            bname = dbanks[t % 2]
            bank, bw = self.fb(bname)
            m = None
            for ic in range(32):
                w = ()
                if ic == 0:
                    w = [self.marks[("dnb_loaded", 2 * t)], (cha, gelu_last),
                         self.marks.get("ysel_free")] + bw
                elif ic == 16:
                    w = [self.marks[("dnb_loaded", 2 * t + 1)]]
                m = self.mm(bank[:, 0:CSEL],
                            dn_bufs[(2 * t + ic // 16) % 4][:, (ic % 16) * 128:(ic % 16 + 1) * 128],
                            h[:, ic * CSEL:(ic + 1) * CSEL],
                            ic == 0, ic == 31, waits=w)
            a = self.dve("tensor_copy", waits=[(chp, m)],
                         out=S["ysel"][:, t * CSEL:(t + 1) * CSEL],
                         in_=bank[:, 0:CSEL])
            ysel_marks.append((chd, a))
            self.fb_release(bname, (chd, a))
            self.marks[("dnb_free", (2 * t) % 4)] = (chp, m)
            self.marks[("dnb_free", (2 * t + 1) % 4)] = (chp, m)
            if 2 * t + 4 < 16:
                load_dn(2 * t + 4)
                load_dn(2 * t + 5)
            self.pop_unit()
        yst_marks = []
        for jh in range(2):
            bank, slot, bw = self.tbank()
            last = None
            for t in range(8):
                last = self.tr(bank[:, t * 128:(t + 1) * 128],
                               S["ysel"][:, t * CSEL + jh * 128:t * CSEL + (jh + 1) * 128],
                               S["eyeb"][:, :],
                               waits=(ysel_marks + bw +
                                      [self.marks.get("yselT_free")]) if t == 0 else ())
            ev = self.dve("tensor_copy", waits=[(chp, last)],
                          out=S["yselT"][:, jh * D:(jh + 1) * D], in_=bank[:, :])
            self.tbank_release(slot, (chd, ev))
            yst_marks.append((chd, ev))
        self.marks["ysel_free"] = (chp, last)
        self.pop_unit()
        rbanks = ["F2", "F3"]
        ridx = 0
        for tt in range(4):
            for dh in range(2):
                bname = rbanks[ridx % 2]
                ridx += 1
                bank, bw = self.fb(bname)
                m = None
                for jh in range(2):
                    w = ()
                    if jh == 0:
                        w = ptw_marks + yst_marks + bw
                    m = self.mm(bank[:, :],
                                S["PTw"][:, jh * GRP + tt * 128:jh * GRP + (tt + 1) * 128],
                                S["yselT"][:, jh * D + dh * 512:jh * D + (dh + 1) * 512],
                                jh == 0, jh == 1, waits=w)
                ochunk = S["out_ffn"][:, tt * D + dh * 512:tt * D + (dh + 1) * 512]
                if e == 0:
                    dA = self.dve("tensor_copy", waits=[(chp, m)], out=ochunk,
                                  in_=bank[:, :])
                else:
                    dA = self.dve("tensor_tensor",
                                  waits=[(chp, m), self.marks.get(("offn", tt, dh))],
                                  out=ochunk, in0=bank[:, :], in1=ochunk, op=ALU.add)
                self.fb_release(bname, (chd, dA))
                self.marks[("offn", tt, dh)] = (chd, dA)
        self.marks["PTw_free"] = (chp, m)
        self.marks["yselT_free"] = (chp, m)
        self.pop_unit()
        if e == N_EXP - 1:
            self.marks["xn_tok_free"] = (chp, m)

    def emit_combine(self, g):
        S, D_, ch = self.sb, self.dram, self.ch
        chp, cha, chd, chpl = ch["pe"], ch["act"], ch["dve"], ch["pool"]
        cho, chx = ch["out"], ch["x"]
        for tt in range(4):
            offw = [self.marks[("offn", tt, 0)], self.marks[("offn", tt, 1)]]
            ot = S["out_ffn"][:, tt * D:(tt + 1) * D]
            lx = self.dma("scalar", "x", S["x0"][:, :],
                          D_["x"].ap()[g * GRP + tt * 128:g * GRP + (tt + 1) * 128, :],
                          waits=[self.marks.get(("x_free", 0))])
            st = S["lnstat"]
            d1 = self.dve("reduce_sum", waits=offw, out=st[:, 0:1], in_=ot,
                          axis=AX.X)
            a1 = self.act(S["sq0"][:, :], ot, AF.Square,
                          waits=offw + [self.marks.get(("sq_free", 0))])
            d2 = self.dve("reduce_sum", waits=[(cha, a1)], out=st[:, 1:2],
                          in_=S["sq0"][:, :], axis=AX.X)
            self.marks[("sq_free", 0)] = (chd, d2)
            d3 = self.dve("tensor_scalar", waits=[(chd, d1)], out=st[:, 2:3],
                          in0=st[:, 0:1], scalar1=1.0 / D, scalar2=None,
                          op0=ALU.mult)
            d4 = self.dve("tensor_scalar", waits=[(chd, d2)], out=st[:, 3:4],
                          in0=st[:, 1:2], scalar1=1.0 / D, scalar2=None,
                          op0=ALU.mult)
            d5 = self.dve("tensor_tensor", waits=[(chd, d3)], out=st[:, 4:5],
                          in0=st[:, 2:3], in1=st[:, 2:3], op=ALU.mult)
            d6 = self.dve("tensor_tensor", waits=[(chd, d5)], out=st[:, 5:6],
                          in0=st[:, 3:4], in1=st[:, 4:5], op=ALU.subtract)
            a2 = self.act(st[:, 6:7], st[:, 5:6], AF.Ln, waits=[(chd, d6)],
                          bias=S["epsv"][:, 0:1])
            a3 = self.act(st[:, 7:8], st[:, 6:7], AF.Exp, waits=[(cha, a2)],
                          scale=-0.5)
            d8 = self.dve("tensor_scalar", waits=[(cha, a3)], out=ot, in0=ot,
                          scalar1=st[:, 2:3], scalar2=st[:, 7:8],
                          op0=ALU.subtract, op1=ALU.mult)
            p1 = self.pool("tensor_tensor", waits=[(chd, d8), (chx, lx)],
                           out=ot, in0=ot, in1=S["x0"][:, :], op=ALU.add)
            self.marks[("x_free", 0)] = (chpl, p1)
            d9 = self.dve("tensor_tensor", waits=[(chpl, p1)], out=ot, in0=ot,
                          in1=S["xn_tok"][:, tt * D:(tt + 1) * D], op=ALU.add)
            bank, slot, bw = self.tbank()
            last = None
            for c in range(8):
                last = self.tr(bank[:, c * 128:(c + 1) * 128],
                               S["attn"][:, c * GRP + tt * 128:c * GRP + (tt + 1) * 128],
                               S["eyeb"][:, :],
                               waits=(self.attn_ready + bw) if c == 0 else ())
            d10 = self.dve("tensor_tensor", waits=[(chp, last), (chd, d9)],
                           out=ot, in0=bank[:, :], in1=ot, op=ALU.add)
            self.tbank_release(slot, (chd, d10))
            stm = self.dma("sync", "out",
                           D_["out"].ap()[g * GRP + tt * 128:g * GRP + (tt + 1) * 128, :],
                           ot, waits=[(chd, d10)])
            self.marks[("offn", tt, 0)] = (cho, stm)
            self.marks[("offn", tt, 1)] = (cho, stm)

    def emit_all(self):
        self.emit_prologue()
        for e in range(4):
            self.emit_conv_expert(e)
        for g in range(N_GRP):
            self.emit_ln(g)
            self.emit_latent(g)
            self.emit_kv(g)
            self.emit_gate(g)
            self.build_attention_units(g)
            for e in range(N_EXP):
                self.emit_ffn(g, e)
                if g == 0 and e < 4:
                    self.emit_conv_expert(e + 4)
            while self._attn_units:
                self.pop_unit()
            self.emit_proj(g)
            self.emit_combine(g)
        self.barrier("sync", [(self.ch["out"], self.ch["out"].n)])

    def finalize(self):
        nc, P, sems = self.nc, self.P, self.sems
        with nc.Block() as block:
            def make(engname):
                ops = [(fn, waits, incs) for (e, fn, waits, incs) in P.ops
                       if e == engname]

                def section(eng, ops=ops):
                    for fn, waits, incs in ops:
                        for chn, v in waits:
                            eng.wait_ge(sems[chn.name], v)
                        if fn is None:
                            continue
                        instr = fn(eng)
                        for chn, k in incs:
                            instr.then_inc(sems[chn.name], k)
                return section
            block.tensor(make("tensor"))
            block.scalar(make("scalar"))
            block.vector(make("vector"))
            block.gpsimd(make("gpsimd"))
            block.sync(make("sync"))


def build_program():
    b = Builder()
    b.declare()
    with contextlib.ExitStack() as es:
        b.alloc(es)
        b.emit_all()
        b.P.check_deadlock()
        b.finalize()
    return b.nc


def _consts():
    import ml_dtypes
    bf16 = ml_dtypes.bfloat16
    eyeb = np.eye(128, dtype=bf16)
    eyef = np.eye(128, dtype=np.float32)
    ii = np.arange(128)[:, None]
    uu = np.arange(896)[None, :]
    maskc = np.where(uu < ii + 384, np.float32(-1e30), np.float32(0.0)).astype(bf16)
    dup64 = np.concatenate([np.eye(64), np.eye(64)], axis=1).astype(bf16)
    jrow = np.broadcast_to(np.arange(CSEL, dtype=np.float32), (128, CSEL)).copy()
    return eyeb, eyef, maskc, dup64, jrow


def kernel(L0_attn_x, L0_attn_Wc_down, L0_attn_Wq_up, L0_attn_Wkv_up,
           L0_attn_Wc_proj, L0_ffn_gate_w, L0_ffn_expert_bias, L0_ffn_w_up,
           L0_ffn_w_down):
    x = np.ascontiguousarray(np.asarray(L0_attn_x, dtype=np.float32))
    wcd = np.ascontiguousarray(np.asarray(L0_attn_Wc_down, dtype=np.float32))
    wq = np.ascontiguousarray(np.asarray(L0_attn_Wq_up, dtype=np.float32))
    wkv = np.ascontiguousarray(np.asarray(L0_attn_Wkv_up, dtype=np.float32))
    wproj = np.ascontiguousarray(np.asarray(L0_attn_Wc_proj, dtype=np.float32))
    gw = np.ascontiguousarray(np.asarray(L0_ffn_gate_w, dtype=np.float32))
    eb = np.ascontiguousarray(np.asarray(L0_ffn_expert_bias, dtype=np.float32))
    wup = np.ascontiguousarray(np.asarray(L0_ffn_w_up, dtype=np.float32))
    wdn = np.ascontiguousarray(np.asarray(L0_ffn_w_down, dtype=np.float32))
    eyeb, eyef, maskc, dup64, jrow = _consts()
    nc = build_program()
    in_maps = []
    for c in range(N_CORES):
        in_maps.append({
            "x": np.ascontiguousarray(x[2 * c:2 * c + 2].reshape(N_TOK, D)),
            "d_wcd": wcd, "d_wq": wq, "d_wkv": wkv, "d_wproj": wproj,
            "gw": gw, "eb": eb, "wup": wup, "wdn": wdn,
            "d_eyeb": eyeb, "d_eyef": eyef, "d_maskc": maskc, "d_dup64": dup64,
            "d_jrow": jrow,
        })
    res = run_bass_kernel_spmd(nc, in_maps, core_ids=list(range(N_CORES)))
    out = np.stack([r["out"] for r in res.results], axis=0)
    return out.reshape(16, 2048, 1024)


# version 16
# speedup vs baseline: 1.0772x; 1.0024x over previous
import contextlib
import numpy as np

import concourse.bass as bass
import concourse.mybir as mybir
from concourse.bass_utils import run_bass_kernel_spmd

F32 = mybir.dt.float32
BF16 = mybir.dt.bfloat16
AF = mybir.ActivationFunctionType
ALU = mybir.AluOpType
AX = mybir.AxisListType

N_CORES = 8
D = 1024
T_SEQ = 2048
N_TOK = 4096
GRP = 512
N_GRP = 8
N_EXP = 8
I_FFN = 4096
LAT = 512
HD = 64
N_PAIR = 8
CSEL = 256
HUGE = 16777216.0


class Ch:
    __slots__ = ("name", "n")

    def __init__(self, name):
        self.name = name
        self.n = 0


class Prog:
    def __init__(self):
        self.ops = []

    def op(self, eng, fn, waits=(), incs=()):
        wd = {}
        for w in waits:
            if w is None:
                continue
            ch, v = w
            if v > 0 and (ch not in wd or wd[ch] < v):
                wd[ch] = v
        rec = []
        for ch, k in incs:
            ch.n += k
            rec.append((ch, k))
        self.ops.append((eng, fn, tuple(wd.items()), tuple(rec)))

    def check_deadlock(self):
        by_eng = {e: [] for e in ("tensor", "scalar", "vector", "gpsimd", "sync")}
        for i, (eng, fn, waits, incs) in enumerate(self.ops):
            by_eng[eng].append((i, waits, incs))
        ptr = {e: 0 for e in by_eng}
        val = {}
        total = sum(len(v) for v in by_eng.values())
        done = 0
        progress = True
        while progress:
            progress = False
            for e, lst in by_eng.items():
                while ptr[e] < len(lst):
                    i, waits, incs = lst[ptr[e]]
                    if any(val.get(ch, 0) < v for ch, v in waits):
                        break
                    for ch, k in incs:
                        val[ch] = val.get(ch, 0) + k
                    ptr[e] += 1
                    done += 1
                    progress = True
        if done != total:
            msgs = []
            for e, lst in by_eng.items():
                if ptr[e] < len(lst):
                    i, waits, incs = lst[ptr[e]]
                    missing = [(ch.name, v, val.get(ch, 0)) for ch, v in waits
                               if val.get(ch, 0) < v]
                    msgs.append(f"{e}@op{i} needs {missing}")
            raise RuntimeError("DEADLOCK: " + "; ".join(msgs))


class Builder:
    def __init__(self):
        self.nc = bass.Bass("TRN2", target_bir_lowering=False, debug=False)
        self.P = Prog()
        self.ch = {n: Ch(n) for n in
                   ("pe", "act", "dve", "pool", "mld", "cvl", "cst", "wup",
                    "wdn", "x", "out")}
        self.sb = {}
        self.ps = {}
        self.dram = {}
        self.marks = {}
        self.tring_free = [None, None]
        self.tring_i = 0
        self.bank_free = {k: None for k in ("F0", "F1", "F2", "F3", "F4", "F5")}
        self._pm = {}
        self._gm = {}

    def declare(self):
        nc, dr = self.nc, self.dram
        dr["x"] = nc.dram_tensor("x", [N_TOK, D], F32, kind="ExternalInput")
        dr["wcd"] = nc.dram_tensor("d_wcd", [D, LAT], F32, kind="ExternalInput")
        dr["wq"] = nc.dram_tensor("d_wq", [LAT, D], F32, kind="ExternalInput")
        dr["wkv"] = nc.dram_tensor("d_wkv", [LAT, 2 * HD], F32, kind="ExternalInput")
        dr["wproj"] = nc.dram_tensor("d_wproj", [D, D], F32, kind="ExternalInput")
        dr["gw"] = nc.dram_tensor("gw", [N_EXP, D], F32, kind="ExternalInput")
        dr["eb"] = nc.dram_tensor("eb", [N_EXP], F32, kind="ExternalInput")
        dr["wup"] = nc.dram_tensor("wup", [N_EXP, I_FFN, D], F32, kind="ExternalInput")
        dr["wdn"] = nc.dram_tensor("wdn", [N_EXP, D, I_FFN], F32, kind="ExternalInput")
        dr["eyeb"] = nc.dram_tensor("d_eyeb", [128, 128], BF16, kind="ExternalInput")
        dr["eyef"] = nc.dram_tensor("d_eyef", [128, 128], F32, kind="ExternalInput")
        dr["maskc"] = nc.dram_tensor("d_maskc", [128, 896], BF16, kind="ExternalInput")
        dr["dup64"] = nc.dram_tensor("d_dup64", [64, 128], BF16, kind="ExternalInput")
        dr["jrow"] = nc.dram_tensor("d_jrow", [128, CSEL], F32,
                                    kind="ExternalInput")
        dr["out"] = nc.dram_tensor("out", [N_TOK, D], F32, kind="ExternalOutput")
        dr["up_s"] = nc.dram_tensor("up_s", [N_EXP, 8, 128, 4096], BF16)
        dr["dn_s"] = nc.dram_tensor("dn_s", [N_EXP, 8, 128, 4096], BF16)

    def alloc(self, es):
        nc, S = self.nc, self.sb

        def sbuf(name, shape, dt):
            S[name] = es.enter_context(nc.sbuf_tensor(name, shape, dt))

        sbuf("eyeb", [128, 128], BF16)
        sbuf("eyef", [128, 128], F32)
        sbuf("maskc", [128, 896], BF16)
        sbuf("dup64", [64, 128], BF16)
        sbuf("ones1f", [1, 128], F32)
        sbuf("onesb", [128, 64], BF16)
        sbuf("onescol", [128, 1], BF16)
        sbuf("gate_wT", [128, 64], BF16)
        sbuf("ebrow", [1, N_EXP], F32)
        sbuf("bias_b", [128, N_EXP], F32)
        sbuf("wcd", [128, 4096], BF16)
        sbuf("wq", [128, 4096], BF16)
        sbuf("wkv", [128, 512], BF16)
        sbuf("wproj", [128, 8192], BF16)
        sbuf("k_dup", [128, T_SEQ], BF16)
        sbuf("v_aug", [128, 16 * 65], BF16)
        sbuf("xn", [128, 8 * GRP], BF16)
        sbuf("latent", [128, 4 * GRP], BF16)
        sbuf("routg", [128, 160], F32)
        sbuf("mrow", [8, GRP], F32)
        sbuf("posrow", [8, GRP], F32)
        sbuf("ones8", [8, GRP], F32)
        sbuf("carry", [8, 1], F32)
        sbuf("ptok", [128, 32], F32)
        sbuf("jrow", [128, CSEL], F32)
        sbuf("xn_tok", [128, 4 * D], BF16)
        sbuf("Pg", [128, 4 * CSEL], BF16)
        sbuf("Pw", [128, 4 * CSEL], BF16)
        sbuf("PTw", [128, 2 * GRP], BF16)
        sbuf("gxn", [128, 8 * CSEL], BF16)
        sbuf("ysel", [128, 8 * CSEL], BF16)
        sbuf("yselT", [128, 2 * D], BF16)
        sbuf("arena", [128, 8192], BF16)
        sbuf("out_ffn", [128, 4 * D], F32)
        sbuf("up_buf0", [128, 2048], BF16)
        sbuf("up_buf1", [128, 2048], BF16)
        sbuf("up_buf2", [128, 2048], BF16)
        sbuf("up_buf3", [128, 2048], BF16)
        sbuf("dn_buf0", [128, 2048], BF16)
        sbuf("dn_buf1", [128, 2048], BF16)
        sbuf("dn_buf2", [128, 2048], BF16)
        sbuf("dn_buf3", [128, 2048], BF16)
        sbuf("tmp0", [128, 128], F32)
        sbuf("attn", [128, 8 * GRP], BF16)
        sbuf("y_g", [128, 8 * GRP], BF16)
        sbuf("q0", [128, GRP], BF16)
        sbuf("pT0", [128, 1024], BF16)
        sbuf("pT1", [128, 1024], BF16)
        sbuf("rz", [128, 512], BF16)
        sbuf("zb", [128, GRP], BF16)
        sbuf("kvb", [128, GRP], BF16)
        sbuf("x0", [128, D], F32)
        sbuf("sq0", [128, D], BF16)
        sbuf("lnstat", [128, 16], F32)
        sbuf("epsv", [128, 1], F32)
        sbuf("conv0", [128, 4096], BF16)
        sbuf("conv1", [128, 4096], BF16)
        sbuf("ev", [128, 4096], BF16)

        PS = self.ps
        for k in ("F0", "F1", "F2", "F3", "F4", "F5"):
            PS[k] = es.enter_context(nc.psum_tensor(k, [128, 512], F32))
        PS["TB"] = es.enter_context(nc.psum_tensor("TB", [128, 2048], BF16))
        self.sems = {n: es.enter_context(nc.semaphore("s_" + n)) for n in self.ch}

    def mm(self, out_ap, lhsT, rhs, start, stop, waits=(), tile_position=None):
        ch = self.ch["pe"]

        def fn(eng, o=out_ap, l=lhsT, r=rhs, a=start, b=stop, tp=tile_position):
            return eng.matmul(o, l, r, start=a, stop=b, tile_position=tp)
        self.P.op("tensor", fn, waits=waits, incs=((ch, 1),))
        return ch.n

    def tr(self, out_ap, in_ap, ident, waits=()):
        ch = self.ch["pe"]

        def fn(eng, o=out_ap, i=in_ap, d=ident):
            return eng.transpose(o, i, d)
        self.P.op("tensor", fn, waits=waits, incs=((ch, 1),))
        return ch.n

    def act(self, out_ap, in_ap, func, waits=(), bias=None, scale=None):
        ch = self.ch["act"]

        def fn(eng, o=out_ap, i=in_ap, f=func, b=bias, s=scale):
            kw = {}
            if b is not None:
                kw["bias"] = b
            if s is not None:
                kw["scale"] = s
            return eng.activation(o, i, f, **kw)
        self.P.op("scalar", fn, waits=waits, incs=((ch, 1),))
        return ch.n

    def acopy(self, out_ap, in_ap, waits=()):
        ch = self.ch["act"]

        def fn(eng, o=out_ap, i=in_ap):
            return eng.copy(o, i)
        self.P.op("scalar", fn, waits=waits, incs=((ch, 1),))
        return ch.n

    def dve(self, kind, waits=(), allow_lp=False, **kw):
        ch = self.ch["dve"]
        nc = self.nc

        def fn(eng, kind=kind, kw=kw, allow_lp=allow_lp):
            if allow_lp:
                with nc.allow_low_precision(reason="bf16 1/Z: feeds bf16 matmul"):
                    return getattr(eng, kind)(**kw)
            return getattr(eng, kind)(**kw)
        self.P.op("vector", fn, waits=waits, incs=((ch, 1),))
        return ch.n

    def pool(self, kind, waits=(), **kw):
        ch = self.ch["pool"]

        def fn(eng, kind=kind, kw=kw):
            return getattr(eng, kind)(**kw)
        self.P.op("gpsimd", fn, waits=waits, incs=((ch, 1),))
        return ch.n

    def dma(self, queue, chname, out_ap, in_ap, waits=()):
        ch = self.ch[chname]

        def fn(eng, o=out_ap, i=in_ap):
            return eng.dma_start(out=o, in_=i)
        self.P.op(queue, fn, waits=waits, incs=((ch, 16),))
        return ch.n

    def barrier(self, eng, waits):
        self.P.op(eng, None, waits=waits)

    def tbank(self):
        slot = self.tring_i % 2
        self.tring_i += 1
        bank = self.ps["TB"][:, slot * 1024:(slot + 1) * 1024]
        w = self.tring_free[slot]
        self.tring_free[slot] = None
        return bank, slot, ([w] if w is not None else [])

    def tbank_release(self, slot, wait):
        self.tring_free[slot] = wait

    def tbank_pair(self):
        if self.tring_i % 2 == 1:
            _, s, w = self.tbank()
            self.tbank_release(s, w[0] if w else None)
        _, s0, w0 = self.tbank()
        _, s1, w1 = self.tbank()
        assert s0 == 0 and s1 == 1
        return w0 + w1

    def fb(self, name):
        w = self.bank_free[name]
        self.bank_free[name] = None
        return self.ps[name], ([w] if w is not None else [])

    def fb_release(self, name, wait):
        self.bank_free[name] = wait

    def emit_prologue(self):
        S, D_, ch = self.sb, self.dram, self.ch
        loads = [
            (S["eyeb"][:, :], D_["eyeb"].ap()),
            (S["eyef"][:, :], D_["eyef"].ap()),
            (S["maskc"][:, :], D_["maskc"].ap()),
            (S["dup64"][:, :], D_["dup64"].ap()),
            (S["jrow"][:, :], D_["jrow"].ap()),
            (S["ysel"][0:N_EXP, 0:D][:, :], D_["gw"].ap()),
            (S["ebrow"][:, :], D_["eb"].ap().rearrange("(o e) -> o e", o=1)),
            (S["wcd"][:, :].rearrange("p (c l) -> p c l", l=LAT),
             D_["wcd"].ap().rearrange("(c p) l -> p c l", p=128)),
            (S["wq"][:, :].rearrange("p (c q) -> p c q", q=D),
             D_["wq"].ap().rearrange("(c p) q -> p c q", p=128)),
            (S["wkv"][:, :].rearrange("p (c k) -> p c k", k=2 * HD),
             D_["wkv"].ap().rearrange("(c p) k -> p c k", p=128)),
            (S["wproj"][:, :].rearrange("p (c o) -> p c o", o=D),
             D_["wproj"].ap().rearrange("(c p) o -> p c o", p=128)),
        ]
        for o, i in loads:
            self.dma("gpsimd", "mld", o, i)
        self.mld_all = ch["mld"].n
        for name, v in (("ones1f", 1.0), ("onesb", 1.0), ("onescol", 1.0),
                        ("ones8", 1.0), ("carry", 0.0),
                        ("epsv", 1e-5)):
            self.pool("memset", ap=S[name][:, :], constant=float(v))
        self.pool("memset", ap=S["v_aug"][:, :], constant=1.0)
        self.pool_init = ch["pool"].n
        mld_w = (ch["mld"], self.mld_all)
        pool_w = (ch["pool"], self.pool_init)

        bank, slot, bw = self.tbank()
        last = None
        for c in range(8):
            last = self.tr(bank[:, c * 8:(c + 1) * 8],
                           S["ysel"][0:N_EXP, 0:D][0:8, c * 128:(c + 1) * 128],
                           S["eyeb"][0:8, 0:8],
                           waits=([mld_w] + bw) if c == 0 else ())
        ev = self.dve("tensor_copy", waits=[(ch["pe"], last)],
                      out=S["gate_wT"][:, :], in_=bank[:, 0:64])
        self.tbank_release(slot, (ch["dve"], ev))
        self.gate_wT_ready = (ch["dve"], ev)

        bank, bw = self.fb("F0")
        m = self.mm(bank[0:128, 0:N_EXP], S["ones1f"][:, :], S["ebrow"][:, :],
                    True, True, waits=[mld_w, pool_w] + bw)
        a = self.acopy(S["bias_b"][:, :], bank[0:128, 0:N_EXP],
                       waits=[(ch["pe"], m)])
        self.fb_release("F0", (ch["act"], a))
        self.bias_b_ready = (ch["act"], a)
        self.conv_free = [None, None]
        self.ev_free = None
        self.conv_mark = {}

    def emit_conv_expert(self, e):
        S, D_, ch = self.sb, self.dram, self.ch
        chp, chd, chc, chs = ch["pe"], ch["dve"], ch["cvl"], ch["cst"]
        mld_w = (ch["mld"], self.mld_all)
        convs = [S["conv0"], S["conv1"]]
        for q in range(8):
            r = q % 2
            ld = self.dma("gpsimd", "cvl",
                          convs[r][:, :].rearrange("p (a h) -> p a h", h=1024),
                          D_["wup"].ap()[e, q * 512:(q + 1) * 512, :].rearrange(
                              "(a p) h -> p a h", p=128),
                          waits=[self.conv_free[r], mld_w])
            for a in range(4):
                s = q * 4 + a
                g, sl = s // 4, s % 4
                hf, s2 = sl // 2, s % 2
                bank, slot, bw = self.tbank()
                last = None
                for c in range(8):
                    last = self.tr(bank[:, c * 128:(c + 1) * 128],
                                   convs[r][:, a * 1024 + c * 128:a * 1024 + (c + 1) * 128],
                                   S["eyeb"][:, :],
                                   waits=([(chc, ld)] + bw) if c == 0 else ())
                evw = [(chp, last)]
                if sl == 0 and self.ev_free is not None:
                    evw.append(self.ev_free)
                out_ap = S["ev"][:, hf * 2048:(hf + 1) * 2048].rearrange(
                    "p (c i) -> p c i", i=256)[:, :, s2 * 128:(s2 + 1) * 128]
                in_ap = bank[:, :].rearrange("p (c i) -> p c i", i=128)
                evv = self.dve("tensor_copy", waits=evw, out=out_ap, in_=in_ap)
                self.tbank_release(slot, (chd, evv))
                if sl == 3:
                    st = self.dma("sync", "cst", D_["up_s"].ap()[e, g], S["ev"][:, :],
                                  waits=[(chd, evv)])
                    self.ev_free = (chs, st)
            self.conv_free[r] = (chp, last)
        self.conv_mark[(e, "up")] = (chs, chs.n)
        for s in range(8):
            r = s % 2
            ld = self.dma("gpsimd", "cvl", convs[r][:, :],
                          D_["wdn"].ap()[e, s * 128:(s + 1) * 128, :],
                          waits=[self.conv_free[r]])
            last_ev = None
            last_tr = None
            for g8 in range(4):
                bank, slot, bw = self.tbank()
                last_tr = None
                for k in range(8):
                    ic = g8 * 8 + k
                    last_tr = self.tr(bank[:, k * 128:(k + 1) * 128],
                                      convs[r][:, ic * 128:(ic + 1) * 128],
                                      S["eyeb"][:, :],
                                      waits=([(chc, ld)] + bw) if k == 0 else ())
                evw = [(chp, last_tr)]
                if g8 == 0 and self.ev_free is not None:
                    evw.append(self.ev_free)
                last_ev = self.dve("tensor_copy", waits=evw,
                                   out=S["ev"][:, g8 * 1024:(g8 + 1) * 1024],
                                   in_=bank[:, :])
                self.tbank_release(slot, (chd, last_ev))
            self.conv_free[r] = (chp, last_tr)
            st = self.dma("sync", "cst", D_["dn_s"].ap()[e, s], S["ev"][:, :],
                          waits=[(chd, last_ev)])
            self.ev_free = (chs, st)
        self.conv_mark[(e, "dn")] = (chs, chs.n)

    def emit_ln(self, g):
        S, D_, ch = self.sb, self.dram, self.ch
        chx, chd, cha, chp = ch["x"], ch["dve"], ch["act"], ch["pe"]
        ev = None
        for tt in range(4):
            r = 0
            xt = S["x0"]
            sq = S["sq0"]
            xtb = tt * D
            ld = self.dma("scalar", "x", xt[:, :],
                          D_["x"].ap()[g * GRP + tt * 128:g * GRP + (tt + 1) * 128, :],
                          waits=[self.marks.get(("x_free", r))])
            ldw = (chx, ld)
            cc = [r * 8 + k for k in range(8)]
            st = S["lnstat"]
            d1 = self.dve("reduce_sum", waits=[ldw], out=st[:, cc[0]:cc[0] + 1],
                          in_=xt[:, :], axis=AX.X)
            a1 = self.act(sq[:, :], xt[:, :], AF.Square,
                          waits=[ldw, self.marks.get(("sq_free", r))])
            d2 = self.dve("reduce_sum", waits=[(cha, a1)], out=st[:, cc[1]:cc[1] + 1],
                          in_=sq[:, :], axis=AX.X)
            self.marks[("sq_free", r)] = (chd, d2)
            d3 = self.dve("tensor_scalar", waits=[(chd, d1)],
                          out=st[:, cc[2]:cc[2] + 1], in0=st[:, cc[0]:cc[0] + 1],
                          scalar1=1.0 / D, scalar2=None, op0=ALU.mult)
            d4 = self.dve("tensor_scalar", waits=[(chd, d2)],
                          out=st[:, cc[3]:cc[3] + 1], in0=st[:, cc[1]:cc[1] + 1],
                          scalar1=1.0 / D, scalar2=None, op0=ALU.mult)
            d5 = self.dve("tensor_tensor", waits=[(chd, d3)],
                          out=st[:, cc[4]:cc[4] + 1], in0=st[:, cc[2]:cc[2] + 1],
                          in1=st[:, cc[2]:cc[2] + 1], op=ALU.mult)
            d6 = self.dve("tensor_tensor", waits=[(chd, d5)],
                          out=st[:, cc[5]:cc[5] + 1], in0=st[:, cc[3]:cc[3] + 1],
                          in1=st[:, cc[4]:cc[4] + 1], op=ALU.subtract)
            a2 = self.act(st[:, cc[6]:cc[6] + 1], st[:, cc[5]:cc[5] + 1], AF.Ln,
                          waits=[(chd, d6)], bias=S["epsv"][:, 0:1])
            a3 = self.act(st[:, cc[7]:cc[7] + 1], st[:, cc[6]:cc[6] + 1], AF.Exp,
                          waits=[(cha, a2)], scale=-0.5)
            d7 = self.dve("tensor_scalar",
                          waits=[(cha, a3), self.marks.get("xn_tok_free")],
                          out=S["xn_tok"][:, xtb:xtb + D],
                          in0=xt[:, :], scalar1=st[:, cc[2]:cc[2] + 1],
                          scalar2=st[:, cc[7]:cc[7] + 1], op0=ALU.subtract,
                          op1=ALU.mult)
            self.marks[("x_free", r)] = (chd, d7)
            bank, slot, bw = self.tbank()
            last = None
            for c in range(8):
                last = self.tr(bank[:, c * 128:(c + 1) * 128],
                               S["xn_tok"][:, xtb + c * 128:xtb + (c + 1) * 128],
                               S["eyeb"][:, :],
                               waits=([(chd, d7)] + bw) if c == 0 else ())
            ev = self.dve(
                "tensor_copy", waits=[(chp, last)],
                out=S["xn"][:, :].rearrange("p (c t) -> p c t", t=GRP)[:, :, tt * 128:(tt + 1) * 128],
                in_=bank[:, :].rearrange("p (c t) -> p c t", t=128))
            self.tbank_release(slot, (chd, ev))
        self.xn_ready = (chd, ev)

    def emit_latent(self, g):
        S, ch = self.sb, self.ch
        chp, cha = ch["pe"], ch["act"]
        names = ["F0", "F1"]
        a = None
        for l in range(4):
            bank, bw = self.fb(names[l % 2])
            m = None
            for c in range(8):
                m = self.mm(bank[:, :],
                            S["wcd"][:, c * 512 + l * 128:c * 512 + (l + 1) * 128],
                            S["xn"][:, c * GRP:(c + 1) * GRP], c == 0, c == 7,
                            waits=([self.xn_ready] + bw) if c == 0 else ())
            a = self.acopy(S["latent"][:, l * GRP:(l + 1) * GRP], bank[:, :],
                           waits=[(chp, m)])
            self.fb_release(names[l % 2], (cha, a))
        self.latent_ready = (cha, a)

    def emit_kv(self, g):
        S, ch = self.sb, self.ch
        chp, cha, chd = ch["pe"], ch["act"], ch["dve"]
        t0 = (g % 4) * GRP
        ow = self.marks.get(("attn_done", 3)) if g == 4 else None
        bank, bw = self.fb("F2")
        m = None
        for l in range(4):
            m = self.mm(bank[:, :], S["wkv"][:, l * 128:(l + 1) * 128],
                        S["latent"][:, l * GRP:(l + 1) * GRP], l == 0, l == 3,
                        waits=([self.latent_ready] + bw) if l == 0 else ())
        a1 = self.acopy(S["kvb"][:, :], bank[:, :],
                        waits=[(chp, m), self.marks.get("kvb_free")])
        self.fb_release("F2", (cha, a1))
        bank3, bw3 = self.fb("F3")
        m2 = self.mm(bank3[:, :], S["dup64"][:, :], S["kvb"][0:64, :], True, True,
                     waits=[(cha, a1)] + bw3)
        a2 = self.acopy(S["k_dup"][:, t0:t0 + GRP], bank3[:, :],
                        waits=[(chp, m2), ow])
        self.fb_release("F3", (cha, a2))
        bank, slot, tbw = self.tbank()
        last = None
        for jj in range(4):
            last = self.tr(bank[:, jj * 64:(jj + 1) * 64],
                           S["kvb"][64:128, jj * 128:(jj + 1) * 128],
                           S["eyeb"][64:128, 64:128],
                           waits=([(cha, a1)] + tbw) if jj == 0 else ())
        j0 = t0 // 128
        vw = [(chp, last), (self.ch["pool"], self.pool_init), ow]
        evv = self.dve("tensor_copy", waits=vw,
                       out=S["v_aug"][:, :].rearrange("p (j c) -> p j c", c=65)[:, j0:j0 + 4, 0:64],
                       in_=bank[:, 0:256].rearrange("p (j c) -> p j c", c=64))
        self.tbank_release(slot, (chd, evv))
        self.marks[("kv_ready", g % 4)] = (cha, a2)
        self.marks[("v_ready", g % 4)] = (chd, evv)
        self.marks["kvb_free"] = (chp, last)

    def emit_gate(self, g):
        S, ch = self.sb, self.ch
        chp, cha, chd = ch["pe"], ch["act"], ch["dve"]
        E = N_EXP
        W = S["tmp0"]
        for tt in range(4):
            b = tt * 40
            R = S["routg"]
            mask1 = R[:, b:b + 8]
            mask2 = R[:, b + 8:b + 16]
            m12 = R[:, b + 16:b + 24]
            wsel = R[:, b + 24:b + 32]
            w1c = R[:, b + 32:b + 33]
            w2c = R[:, b + 33:b + 34]
            bank, bw = self.fb("F3")
            m = None
            for c in range(8):
                m = self.mm(bank[0:128, 0:E],
                            S["xn"][:, c * GRP + tt * 128:c * GRP + (tt + 1) * 128],
                            S["gate_wT"][:, c * E:(c + 1) * E], c == 0, c == 7,
                            waits=([self.xn_ready, self.gate_wT_ready,
                                    self.bias_b_ready] + bw) if c == 0 else ())
            wg = W[:, 0:E]
            e_t = W[:, E:2 * E]
            w_ = W[:, 2 * E:3 * E]
            m1 = W[:, 3 * E:3 * E + 1]
            nm = W[:, 3 * E + 1:3 * E + 2]
            w2 = W[:, 5 * E:6 * E]
            m2 = W[:, 6 * E:6 * E + 1]
            dsum = W[:, 9 * E:9 * E + 1]
            rd = W[:, 9 * E + 1:9 * E + 2]
            Z = W[:, 9 * E + 2:9 * E + 3]
            rZ = W[:, 9 * E + 3:9 * E + 4]
            t8 = W[:, 11 * E:12 * E]
            d0 = self.dve("tensor_tensor",
                          waits=[(chp, m), self.marks.get(("tmp_free", 0)),
                                 self.marks.get(("routg_free", tt))],
                          out=wg, in0=bank[0:128, 0:E], in1=S["bias_b"][:, :],
                          op=ALU.add)
            self.fb_release("F3", (chd, d0))
            dm = self.dve("reduce_max", waits=[(chd, d0)], out=nm, in_=wg,
                          axis=AX.X, negate=True)
            a1 = self.act(e_t, wg, AF.Exp, waits=[(chd, dm)], bias=nm)
            d2 = self.dve("reduce_sum", waits=[(cha, a1)], out=Z, in_=e_t, axis=AX.X)
            d3 = self.dve("reciprocal", waits=[(chd, d2)], out=rZ, in_=Z)
            d4 = self.dve("tensor_scalar", waits=[(chd, d3)], out=w_, in0=e_t,
                          scalar1=rZ, scalar2=None, op0=ALU.mult)
            d5 = self.dve("reduce_max", waits=[(chd, d4)], out=m1, in_=w_, axis=AX.X)
            d6 = self.dve("tensor_scalar", waits=[(chd, d5)], out=mask1, in0=w_,
                          scalar1=m1, scalar2=None, op0=ALU.is_ge)
            d7 = self.dve("scalar_tensor_tensor", waits=[(chd, d6)], out=w2,
                          in0=mask1, scalar=-1e30, in1=w_, op0=ALU.mult, op1=ALU.add)
            d8 = self.dve("reduce_max", waits=[(chd, d7)], out=m2, in_=w2, axis=AX.X)
            d9 = self.dve("tensor_scalar", waits=[(chd, d8)], out=mask2, in0=w2,
                          scalar1=m2, scalar2=None, op0=ALU.is_ge)
            d10 = self.dve("tensor_tensor", waits=[(chd, d9)], out=m12, in0=mask1,
                           in1=mask2, op=ALU.add)
            d11 = self.dve("tensor_tensor", waits=[(chd, d8)], out=dsum, in0=m1,
                           in1=m2, op=ALU.add)
            d12 = self.dve("tensor_scalar", waits=[(chd, d11)], out=dsum, in0=dsum,
                           scalar1=1e-6, scalar2=None, op0=ALU.add)
            d13 = self.dve("reciprocal", waits=[(chd, d12)], out=rd, in_=dsum)
            d14 = self.dve("tensor_tensor", waits=[(chd, d13)], out=w1c, in0=m1,
                           in1=rd, op=ALU.mult)
            d15 = self.dve("tensor_tensor", waits=[(chd, d14)], out=w2c, in0=m2,
                           in1=rd, op=ALU.mult)
            d16 = self.dve("tensor_scalar", waits=[(chd, d15)], out=t8, in0=mask1,
                           scalar1=w1c, scalar2=None, op0=ALU.mult)
            d17 = self.dve("tensor_scalar", waits=[(chd, d16)], out=wsel, in0=mask2,
                           scalar1=w2c, scalar2=None, op0=ALU.mult)
            d18 = self.dve("tensor_tensor", waits=[(chd, d17)], out=wsel, in0=wsel,
                           in1=t8, op=ALU.add)
            self.marks[("tmp_free", 0)] = (chd, d18)
            self.marks[("gatechain", tt)] = (chd, d18)
        bank, slot, bw = self.tbank()
        b32 = bank[:, :].bitcast(F32)
        last = None
        for tt in range(4):
            last = self.tr(b32[0:8, tt * 128:(tt + 1) * 128],
                           S["routg"][:, tt * 40 + 16:tt * 40 + 24], S["eyef"][:, :],
                           waits=([self.marks[("gatechain", 3)],
                                   self.marks.get("mrow_free")] + bw) if tt == 0 else ())
        am = self.acopy(S["mrow"][:, :], b32[0:8, 0:512], waits=[(chp, last)])
        self.tbank_release(slot, (cha, am))
        s1 = self.dve("tensor_tensor_scan",
                      waits=[(cha, am), self.marks.get("carry_free")],
                      out=S["posrow"][:, :], data0=S["ones8"][:, :],
                      data1=S["mrow"][:, :],
                      initial=0.0, op0=ALU.mult, op1=ALU.add)
        s2 = self.dve("tensor_copy", waits=[(chd, s1)], out=S["carry"][:, :],
                      in_=S["posrow"][:, 511:512])
        self.marks["carry_free"] = (chd, s2)
        s3 = self.dve("tensor_tensor", waits=[(chd, s2)], out=S["mrow"][:, :],
                      in0=S["posrow"][:, :], in1=S["mrow"][:, :], op=ALU.subtract)
        mr_last = None
        for tt in range(4):
            bank, slot, bw = self.tbank()
            b32 = bank[:, :].bitcast(F32)
            trp = self.tr(b32[:, 0:8], S["mrow"][0:8, tt * 128:(tt + 1) * 128],
                          S["eyef"][0:8, 0:8],
                          waits=[(chd, s3), self.marks.get("ptok_free")] + bw)
            ap = self.acopy(S["ptok"][:, tt * 8:(tt + 1) * 8], b32[:, 0:8],
                            waits=[(chp, trp)])
            self.tbank_release(slot, (cha, ap))
            mr_last = trp
        self.marks["mrow_free"] = (chp, mr_last)
        self.ptok_ready = (cha, ap)

    def tbank32(self, bank):
        return bank[:, :].bitcast(F32)[:, 0:512]

    def build_attention_units(self, g):
        S, ch = self.sb, self.ch
        chp, cha, chd = ch["pe"], ch["act"], ch["dve"]
        t0 = (g % 4) * GRP
        J = t0 // 128 + 4
        diag0 = t0 // 128
        kvw = [self.marks[("kv_ready", q)] for q in range(g % 4 + 1)]
        vw = [self.marks[("v_ready", q)] for q in range(g % 4 + 1)]
        units = []
        st = {"y_marks": []}

        def q_unit(p):
            def fn():
                qb = S["q0"]
                bank, slot, bw = self.tbank()
                b32 = self.tbank32(bank)
                m = None
                for l in range(4):
                    m = self.mm(b32, S["wq"][:, l * D + p * 128:l * D + (p + 1) * 128],
                                S["latent"][:, l * GRP:(l + 1) * GRP], l == 0, l == 3,
                                waits=([self.latent_ready,
                                        self.marks.get(("q_free", 0))] + bw)
                                if l == 0 else ())
                aq = self.acopy(qb[:, :], b32, waits=[(chp, m)])
                self.tbank_release(slot, (cha, aq))
                st[("qw", p)] = (cha, aq)
            return fn

        def score_unit(p, j):
            def fn():
                qb = S["q0"]
                pT = S["pT0"] if j % 2 == 0 else S["pT1"]
                bw = self.tbank_pair()
                TBf = self.ps["TB"][:, :].bitcast(F32)
                bA = TBf[:, 0:512]
                bB = TBf[:, 512:1024]
                mA = self.mm(bA, S["k_dup"][0:64, j * 128:(j + 1) * 128],
                             qb[0:64, :], True, True,
                             waits=[st[("qw", p)]] + kvw + bw)
                mB = self.mm(bB, S["k_dup"][64:128, j * 128:(j + 1) * 128],
                             qb[64:128, :], True, True)
                expw = [(chp, mB)]
                if j >= diag0:
                    o = (j - diag0) * 128
                    msl = S["maskc"][:, 384 - o:896 - o]
                    dA = self.dve("tensor_tensor", waits=[(chp, mA)], out=bA,
                                  in0=bA, in1=msl, op=ALU.add)
                    dB = self.dve("tensor_tensor", waits=[(chp, mB)], out=bB,
                                  in0=bB, in1=msl, op=ALU.add)
                    expw = [(chd, dB)]
                pfree = self.marks.get(("pT_free", j % 2))
                eA = self.act(pT[:, 0:1024], TBf[:, 0:1024], AF.Exp,
                              waits=expw + [pfree], scale=0.125)
                self.tbank_release(0, (cha, eA))
                self.tbank_release(1, (cha, eA))
                st[("exp", p, j)] = [(cha, eA)]
            return fn

        def av_unit(p, j):
            def fn():
                jr = j % 2
                pT = S["pT0"] if jr == 0 else S["pT1"]
                avw = list(st.pop(("exp", p, j)))
                if j == 0:
                    _, bw2 = self.fb("F4")
                    _, bw3 = self.fb("F5")
                    avw = avw + vw + bw2 + bw3
                m1_ = self.mm(self.ps["F4"][0:65, :],
                              S["v_aug"][:, j * 65:j * 65 + 65], pT[:, 0:512],
                              j == 0, j == J - 1, waits=avw)
                m2_ = self.mm(self.ps["F5"][64:128, :],
                              S["v_aug"][:, j * 65:j * 65 + 64], pT[:, 512:1024],
                              j == 0, j == J - 1, tile_position=(0, 64))
                m3 = self.mm(self.ps["F5"][0:1, :], S["onescol"][:, :],
                             pT[:, 512:1024], j == 0, j == J - 1)
                self.marks[("pT_free", jr)] = (chp, m3)
                st[("av", p)] = (chp, m3)
            return fn

        def norm_a_unit(p):
            def fn():
                m3 = st[("av", p)]
                self.marks[("q_free", 0)] = m3
                d1 = self.dve("reciprocal", waits=[m3, self.marks.get("rz_free")],
                              allow_lp=True,
                              out=S["rz"][64:65, 0:512], in_=self.ps["F4"][64:65, :])
                d2 = self.dve("reciprocal", waits=[m3], allow_lp=True,
                              out=S["rz"][0:1, 0:512], in_=self.ps["F5"][0:1, :])
                st[("rec", p)] = (d1, d2)
            return fn

        def norm_b_unit(p):
            def fn():
                d1, d2 = st.pop(("rec", p))
                bkA, slA, bwA = self.tbank()
                zA = self.tbank32(bkA)
                bkB, slB, bwB = self.tbank()
                zB = self.tbank32(bkB)
                mzA = self.mm(zA[0:64, :], S["onesb"][64:65, 0:64],
                              S["rz"][64:65, 0:512], True, True,
                              waits=[(chd, d1)] + bwA)
                mzB = self.mm(zB[64:128, :], S["onesb"][0:1, 0:64],
                              S["rz"][0:1, 0:512], True, True,
                              waits=[(chd, d2)] + bwB, tile_position=(0, 64))
                self.marks["rz_free"] = (chp, mzB)
                azA = self.acopy(S["zb"][0:64, :], zA[0:64, :],
                                 waits=[(chp, mzA), self.marks.get("zb_free")])
                azB = self.acopy(S["zb"][64:128, :], zB[64:128, :],
                                 waits=[(chp, mzB)])
                self.tbank_release(slA, (cha, azA))
                self.tbank_release(slB, (cha, azB))
                dyA = self.dve("tensor_tensor", waits=[(cha, azA)],
                               out=S["y_g"][0:64, p * GRP:(p + 1) * GRP],
                               in0=self.ps["F4"][0:64, :], in1=S["zb"][0:64, :],
                               op=ALU.mult)
                dyB = self.dve("tensor_tensor", waits=[(cha, azB)],
                               out=S["y_g"][64:128, p * GRP:(p + 1) * GRP],
                               in0=self.ps["F5"][64:128, :], in1=S["zb"][64:128, :],
                               op=ALU.mult)
                self.marks["zb_free"] = (chd, dyB)
                self.fb_release("F4", (chd, dyA))
                self.fb_release("F5", (chd, dyB))
                st["y_marks"] += [(chd, dyA), (chd, dyB)]
                if g % 4 == 3 and p == N_PAIR - 1:
                    self.marks[("attn_done", 3)] = st[("av", p)]
            return fn

        for p in range(N_PAIR):
            units.append(q_unit(p))
            for j in range(J):
                units.append(score_unit(p, j))
                units.append(av_unit(p, j))
            units.append(norm_a_unit(p))
            units.append(norm_b_unit(p))
        self._attn_units = units
        self._attn_state = st
        self._slots_left = N_EXP * 28
        return units

    def pop_unit(self):
        n = self._slots_left
        self._slots_left = max(1, n - 1)
        import math
        k = max(1, math.ceil(len(self._attn_units) / max(1, n)))
        for _ in range(min(k, len(self._attn_units))):
            self._attn_units.pop(0)()

    def emit_proj(self, g):
        S, ch = self.sb, self.ch
        chp, cha = ch["pe"], ch["act"]
        y_marks = self._attn_state["y_marks"]
        names = ["F4", "F5"]
        attn_marks = []
        for ho in range(8):
            bank, bw = self.fb(names[ho % 2])
            m = None
            for c in range(8):
                m = self.mm(bank[:, :],
                            S["wproj"][:, c * D + ho * 128:c * D + (ho + 1) * 128],
                            S["y_g"][:, c * GRP:(c + 1) * GRP], c == 0, c == 7,
                            waits=(y_marks + bw) if c == 0 else ())
            a = self.acopy(S["attn"][:, ho * GRP:(ho + 1) * GRP], bank[:, :],
                           waits=[(chp, m)])
            self.fb_release(names[ho % 2], (cha, a))
            attn_marks.append((cha, a))
        self.attn_ready = attn_marks

    def emit_pbuild(self, g, e):
        S, ch = self.sb, self.ch
        chd = ch["dve"]
        W = S["tmp0"]
        p_marks = []
        pw_marks = []
        for tt in range(4):
            b = tt * 40
            qcol = W[:, 14 * N_EXP:14 * N_EXP + 1]
            q1 = self.dve("tensor_scalar",
                          waits=[self.ptok_ready, self.marks.get(("tmp_free", 0)),
                                 self.marks.get(("Pg_free", tt))],
                          out=qcol, in0=S["ptok"][:, tt * 8 + e:tt * 8 + e + 1],
                          scalar1=4096.0, scalar2=None, op0=ALU.add)
            q2 = self.dve("scalar_tensor_tensor", waits=[(chd, q1)], out=qcol,
                          in0=S["routg"][:, b + 16 + e:b + 17 + e], scalar=-4096.0,
                          in1=qcol, op0=ALU.mult, op1=ALU.add)
            q3 = self.dve("tensor_scalar", waits=[(chd, q2)],
                          out=S["Pg"][:, tt * CSEL:(tt + 1) * CSEL],
                          in0=S["jrow"][:, :], scalar1=qcol, scalar2=None,
                          op0=ALU.is_equal)
            q4 = self.dve("tensor_scalar", waits=[(chd, q3)],
                          out=S["Pw"][:, tt * CSEL:(tt + 1) * CSEL],
                          in0=S["Pg"][:, tt * CSEL:(tt + 1) * CSEL],
                          scalar1=S["routg"][:, b + 24 + e:b + 25 + e],
                          scalar2=None, op0=ALU.mult)
            self.marks[("tmp_free", 0)] = (chd, q2)
            p_marks.append((chd, q3))
            pw_marks.append((chd, q4))
        self._pm[(g, e)] = (p_marks, pw_marks)

    def emit_permute(self, g, e):
        S, ch = self.sb, self.ch
        chp, cha, chd = ch["pe"], ch["act"], ch["dve"]
        p_marks, pw_marks = self._pm[(g, e)]
        pbanks = ["F0", "F1"]
        gxn_marks = []
        for dch in range(8):
            bank, bw = self.fb(pbanks[dch % 2])
            m = None
            for tt in range(4):
                w = ()
                if tt == 0:
                    w = p_marks + bw + [self.xn_ready]
                m = self.mm(bank[:, 0:CSEL],
                            S["xn_tok"][:, tt * D + dch * 128:tt * D + (dch + 1) * 128],
                            S["Pg"][:, tt * CSEL:(tt + 1) * CSEL],
                            tt == 0, tt == 3, waits=w)
            a = self.dve("tensor_copy", waits=[(chp, m)],
                         out=S["gxn"][:, dch * CSEL:(dch + 1) * CSEL],
                         in_=bank[:, 0:CSEL])
            self.fb_release(pbanks[dch % 2], (chd, a))
            gxn_marks.append((chd, a))
        self.marks["xn_tok_used"] = (chp, m)
        self.pop_unit()
        self._gm[(g, e)] = gxn_marks

    def emit_ffn(self, g, e):
        S, D_, ch = self.sb, self.dram, self.ch
        chp, cha, chd = ch["pe"], ch["act"], ch["dve"]
        chu, chn = ch["wup"], ch["wdn"]
        h = S["arena"]
        conv_u = self.conv_mark[(e, "up")]
        conv_d = self.conv_mark[(e, "dn")]
        up_bufs = [S["up_buf0"], S["up_buf1"], S["up_buf2"], S["up_buf3"]]
        dn_bufs = [S["dn_buf0"], S["dn_buf1"], S["dn_buf2"], S["dn_buf3"]]

        def load_up(k):
            gg, hf = k // 2, k % 2
            ld = self.dma("gpsimd", "wup", up_bufs[k % 4][:, :],
                          D_["up_s"].ap()[e, gg][:, hf * 2048:(hf + 1) * 2048],
                          waits=[self.marks.get(("upb_free", k % 4)), conv_u])
            self.marks[("upb_loaded", k)] = (chu, ld)

        def load_dn(k):
            t, hf = k // 2, k % 2
            ld = self.dma("sync", "wdn", dn_bufs[k % 4][:, :],
                          D_["dn_s"].ap()[e, t][:, hf * 2048:(hf + 1) * 2048],
                          waits=[self.marks.get(("dnb_free", k % 4)), conv_d])
            self.marks[("dnb_loaded", k)] = (chn, ld)

        for k in range(4):
            load_up(k)
        if (g, e) not in self._pm:
            self.emit_pbuild(g, e)
        p_marks, pw_marks = self._pm[(g, e)]
        if (g, e) not in self._gm:
            self.emit_permute(g, e)
        gxn_marks = self._gm.pop((g, e))
        ptw_marks = []
        for jh in range(2):
            bank, slot, bw = self.tbank()
            last = None
            for tt in range(4):
                last = self.tr(bank[:, tt * 128:(tt + 1) * 128],
                               S["Pw"][:, tt * CSEL + jh * 128:tt * CSEL + (jh + 1) * 128],
                               S["eyeb"][:, :],
                               waits=(pw_marks + bw +
                                      [self.marks.get("PTw_free")]) if tt == 0 else ())
            ev = self.dve("tensor_copy", waits=[(chp, last)],
                          out=S["PTw"][:, jh * GRP:(jh + 1) * GRP],
                          in_=bank[:, 0:512])
            self.tbank_release(slot, (chd, ev))
            ptw_marks.append((chd, ev))
            self.marks[("Pg_free", 10 + jh)] = (chp, last)
        for tt in range(4):
            self.marks[("Pg_free", tt)] = (chp, last)
        self._pm.pop((g, e))
        if e + 1 < N_EXP:
            self.emit_pbuild(g, e + 1)
        self.pop_unit()
        ubanks = ["F0", "F1"]
        gelu_last = None
        for k in range(16):
            for it2 in range(2):
                itile = k * 2 + it2
                bname = ubanks[itile % 2]
                bank, bw = self.fb(bname)
                m = None
                for c in range(8):
                    w = ()
                    if c == 0:
                        w = [self.marks[("upb_loaded", k)]] + gxn_marks + bw
                    m = self.mm(bank[:, 0:CSEL],
                                up_bufs[k % 4][:, c * 256 + it2 * 128:
                                               c * 256 + it2 * 128 + 128],
                                S["gxn"][:, c * CSEL:(c + 1) * CSEL],
                                c == 0, c == 7, waits=w)
                a = self.act(h[:, itile * CSEL:(itile + 1) * CSEL], bank[:, 0:CSEL],
                             AF.Gelu, waits=[(chp, m)])
                self.fb_release(bname, (cha, a))
                gelu_last = a
            self.marks[("upb_free", k % 4)] = (chp, m)
            if k + 4 < 16:
                load_up(k + 4)
            self.pop_unit()
        for k in range(4):
            load_dn(k)
        dbanks = ["F2", "F3"]
        ysel_marks = []
        for t in range(8):
            bname = dbanks[t % 2]
            bank, bw = self.fb(bname)
            m = None
            for ic in range(32):
                w = ()
                if ic == 0:
                    w = [self.marks[("dnb_loaded", 2 * t)], (cha, gelu_last),
                         self.marks.get("ysel_free")] + bw
                elif ic == 16:
                    w = [self.marks[("dnb_loaded", 2 * t + 1)]]
                m = self.mm(bank[:, 0:CSEL],
                            dn_bufs[(2 * t + ic // 16) % 4][:, (ic % 16) * 128:(ic % 16 + 1) * 128],
                            h[:, ic * CSEL:(ic + 1) * CSEL],
                            ic == 0, ic == 31, waits=w)
            a = self.dve("tensor_copy", waits=[(chp, m)],
                         out=S["ysel"][:, t * CSEL:(t + 1) * CSEL],
                         in_=bank[:, 0:CSEL])
            ysel_marks.append((chd, a))
            self.fb_release(bname, (chd, a))
            self.marks[("dnb_free", (2 * t) % 4)] = (chp, m)
            self.marks[("dnb_free", (2 * t + 1) % 4)] = (chp, m)
            if 2 * t + 4 < 16:
                load_dn(2 * t + 4)
                load_dn(2 * t + 5)
            self.pop_unit()
            if t == 3 and e + 1 < N_EXP:
                self.emit_permute(g, e + 1)
        yst_marks = []
        for jh in range(2):
            bank, slot, bw = self.tbank()
            last = None
            for t in range(8):
                last = self.tr(bank[:, t * 128:(t + 1) * 128],
                               S["ysel"][:, t * CSEL + jh * 128:t * CSEL + (jh + 1) * 128],
                               S["eyeb"][:, :],
                               waits=(ysel_marks + bw +
                                      [self.marks.get("yselT_free")]) if t == 0 else ())
            ev = self.dve("tensor_copy", waits=[(chp, last)],
                          out=S["yselT"][:, jh * D:(jh + 1) * D], in_=bank[:, :])
            self.tbank_release(slot, (chd, ev))
            yst_marks.append((chd, ev))
        self.marks["ysel_free"] = (chp, last)
        self.pop_unit()
        rbanks = ["F2", "F3"]
        ridx = 0
        for tt in range(4):
            for dh in range(2):
                bname = rbanks[ridx % 2]
                ridx += 1
                bank, bw = self.fb(bname)
                m = None
                for jh in range(2):
                    w = ()
                    if jh == 0:
                        w = ptw_marks + yst_marks + bw
                    m = self.mm(bank[:, :],
                                S["PTw"][:, jh * GRP + tt * 128:jh * GRP + (tt + 1) * 128],
                                S["yselT"][:, jh * D + dh * 512:jh * D + (dh + 1) * 512],
                                jh == 0, jh == 1, waits=w)
                ochunk = S["out_ffn"][:, tt * D + dh * 512:tt * D + (dh + 1) * 512]
                if e == 0:
                    dA = self.dve("tensor_copy", waits=[(chp, m)], out=ochunk,
                                  in_=bank[:, :])
                else:
                    dA = self.dve("tensor_tensor",
                                  waits=[(chp, m), self.marks.get(("offn", tt, dh))],
                                  out=ochunk, in0=bank[:, :], in1=ochunk, op=ALU.add)
                self.fb_release(bname, (chd, dA))
                self.marks[("offn", tt, dh)] = (chd, dA)
        self.marks["PTw_free"] = (chp, m)
        self.marks["yselT_free"] = (chp, m)
        self.pop_unit()
        if e == N_EXP - 1:
            self.marks["xn_tok_free"] = (chp, m)

    def emit_combine(self, g):
        S, D_, ch = self.sb, self.dram, self.ch
        chp, cha, chd, chpl = ch["pe"], ch["act"], ch["dve"], ch["pool"]
        cho, chx = ch["out"], ch["x"]
        for tt in range(4):
            offw = [self.marks[("offn", tt, 0)], self.marks[("offn", tt, 1)]]
            ot = S["out_ffn"][:, tt * D:(tt + 1) * D]
            lx = self.dma("scalar", "x", S["x0"][:, :],
                          D_["x"].ap()[g * GRP + tt * 128:g * GRP + (tt + 1) * 128, :],
                          waits=[self.marks.get(("x_free", 0))])
            st = S["lnstat"]
            d1 = self.dve("reduce_sum", waits=offw, out=st[:, 0:1], in_=ot,
                          axis=AX.X)
            a1 = self.act(S["sq0"][:, :], ot, AF.Square,
                          waits=offw + [self.marks.get(("sq_free", 0))])
            d2 = self.dve("reduce_sum", waits=[(cha, a1)], out=st[:, 1:2],
                          in_=S["sq0"][:, :], axis=AX.X)
            self.marks[("sq_free", 0)] = (chd, d2)
            d3 = self.dve("tensor_scalar", waits=[(chd, d1)], out=st[:, 2:3],
                          in0=st[:, 0:1], scalar1=1.0 / D, scalar2=None,
                          op0=ALU.mult)
            d4 = self.dve("tensor_scalar", waits=[(chd, d2)], out=st[:, 3:4],
                          in0=st[:, 1:2], scalar1=1.0 / D, scalar2=None,
                          op0=ALU.mult)
            d5 = self.dve("tensor_tensor", waits=[(chd, d3)], out=st[:, 4:5],
                          in0=st[:, 2:3], in1=st[:, 2:3], op=ALU.mult)
            d6 = self.dve("tensor_tensor", waits=[(chd, d5)], out=st[:, 5:6],
                          in0=st[:, 3:4], in1=st[:, 4:5], op=ALU.subtract)
            a2 = self.act(st[:, 6:7], st[:, 5:6], AF.Ln, waits=[(chd, d6)],
                          bias=S["epsv"][:, 0:1])
            a3 = self.act(st[:, 7:8], st[:, 6:7], AF.Exp, waits=[(cha, a2)],
                          scale=-0.5)
            d8 = self.dve("tensor_scalar", waits=[(cha, a3)], out=ot, in0=ot,
                          scalar1=st[:, 2:3], scalar2=st[:, 7:8],
                          op0=ALU.subtract, op1=ALU.mult)
            p1 = self.pool("tensor_tensor", waits=[(chd, d8), (chx, lx)],
                           out=ot, in0=ot, in1=S["x0"][:, :], op=ALU.add)
            self.marks[("x_free", 0)] = (chpl, p1)
            d9 = self.dve("tensor_tensor", waits=[(chpl, p1)], out=ot, in0=ot,
                          in1=S["xn_tok"][:, tt * D:(tt + 1) * D], op=ALU.add)
            bank, slot, bw = self.tbank()
            last = None
            for c in range(8):
                last = self.tr(bank[:, c * 128:(c + 1) * 128],
                               S["attn"][:, c * GRP + tt * 128:c * GRP + (tt + 1) * 128],
                               S["eyeb"][:, :],
                               waits=(self.attn_ready + bw) if c == 0 else ())
            d10 = self.dve("tensor_tensor", waits=[(chp, last), (chd, d9)],
                           out=ot, in0=bank[:, :], in1=ot, op=ALU.add)
            self.tbank_release(slot, (chd, d10))
            stm = self.dma("sync", "out",
                           D_["out"].ap()[g * GRP + tt * 128:g * GRP + (tt + 1) * 128, :],
                           ot, waits=[(chd, d10)])
            self.marks[("offn", tt, 0)] = (cho, stm)
            self.marks[("offn", tt, 1)] = (cho, stm)

    def emit_all(self):
        self.emit_prologue()
        for e in range(4):
            self.emit_conv_expert(e)
        for g in range(N_GRP):
            self.emit_ln(g)
            self.emit_latent(g)
            self.emit_kv(g)
            self.emit_gate(g)
            self.build_attention_units(g)
            for e in range(N_EXP):
                self.emit_ffn(g, e)
                if g == 0 and e < 4:
                    self.emit_conv_expert(e + 4)
            while self._attn_units:
                self.pop_unit()
            self.emit_proj(g)
            self.emit_combine(g)
        self.barrier("sync", [(self.ch["out"], self.ch["out"].n)])

    def finalize(self):
        nc, P, sems = self.nc, self.P, self.sems
        with nc.Block() as block:
            def make(engname):
                ops = [(fn, waits, incs) for (e, fn, waits, incs) in P.ops
                       if e == engname]

                def section(eng, ops=ops):
                    for fn, waits, incs in ops:
                        for chn, v in waits:
                            eng.wait_ge(sems[chn.name], v)
                        if fn is None:
                            continue
                        instr = fn(eng)
                        for chn, k in incs:
                            instr.then_inc(sems[chn.name], k)
                return section
            block.tensor(make("tensor"))
            block.scalar(make("scalar"))
            block.vector(make("vector"))
            block.gpsimd(make("gpsimd"))
            block.sync(make("sync"))


def build_program():
    b = Builder()
    b.declare()
    with contextlib.ExitStack() as es:
        b.alloc(es)
        b.emit_all()
        b.P.check_deadlock()
        b.finalize()
    return b.nc


def _consts():
    import ml_dtypes
    bf16 = ml_dtypes.bfloat16
    eyeb = np.eye(128, dtype=bf16)
    eyef = np.eye(128, dtype=np.float32)
    ii = np.arange(128)[:, None]
    uu = np.arange(896)[None, :]
    maskc = np.where(uu < ii + 384, np.float32(-1e30), np.float32(0.0)).astype(bf16)
    dup64 = np.concatenate([np.eye(64), np.eye(64)], axis=1).astype(bf16)
    jrow = np.broadcast_to(np.arange(CSEL, dtype=np.float32), (128, CSEL)).copy()
    return eyeb, eyef, maskc, dup64, jrow


def kernel(L0_attn_x, L0_attn_Wc_down, L0_attn_Wq_up, L0_attn_Wkv_up,
           L0_attn_Wc_proj, L0_ffn_gate_w, L0_ffn_expert_bias, L0_ffn_w_up,
           L0_ffn_w_down):
    x = np.ascontiguousarray(np.asarray(L0_attn_x, dtype=np.float32))
    wcd = np.ascontiguousarray(np.asarray(L0_attn_Wc_down, dtype=np.float32))
    wq = np.ascontiguousarray(np.asarray(L0_attn_Wq_up, dtype=np.float32))
    wkv = np.ascontiguousarray(np.asarray(L0_attn_Wkv_up, dtype=np.float32))
    wproj = np.ascontiguousarray(np.asarray(L0_attn_Wc_proj, dtype=np.float32))
    gw = np.ascontiguousarray(np.asarray(L0_ffn_gate_w, dtype=np.float32))
    eb = np.ascontiguousarray(np.asarray(L0_ffn_expert_bias, dtype=np.float32))
    wup = np.ascontiguousarray(np.asarray(L0_ffn_w_up, dtype=np.float32))
    wdn = np.ascontiguousarray(np.asarray(L0_ffn_w_down, dtype=np.float32))
    eyeb, eyef, maskc, dup64, jrow = _consts()
    nc = build_program()
    in_maps = []
    for c in range(N_CORES):
        in_maps.append({
            "x": np.ascontiguousarray(x[2 * c:2 * c + 2].reshape(N_TOK, D)),
            "d_wcd": wcd, "d_wq": wq, "d_wkv": wkv, "d_wproj": wproj,
            "gw": gw, "eb": eb, "wup": wup, "wdn": wdn,
            "d_eyeb": eyeb, "d_eyef": eyef, "d_maskc": maskc, "d_dup64": dup64,
            "d_jrow": jrow,
        })
    res = run_bass_kernel_spmd(nc, in_maps, core_ids=list(range(N_CORES)))
    out = np.stack([r["out"] for r in res.results], axis=0)
    return out.reshape(16, 2048, 1024)
